# version 07
# baseline (speedup 1.0000x reference)
import numpy as np
import ml_dtypes
import concourse.bass as bass
import concourse.mybir as mybir
from concourse.alu_op_type import AluOpType
from concourse.bass_utils import run_bass_kernel_spmd
from contextlib import ExitStack

F32 = mybir.dt.float32
BF16 = mybir.dt.bfloat16
AF = mybir.ActivationFunctionType
AX = mybir.AxisListType
OP = AluOpType

K23 = float(1.5 * 2.0 ** 23)
K52 = float(1.5 * 2.0 ** 52)

T, D, I, S = 4096, 1024, 3072, 2048
NSEQ, ND, NI, NT, H, HS = 2, 8, 24, 32, 16, 64
EPS = 1e-5

ENGS = ("sync", "tensor", "vector", "scalar", "gpsimd")


class Ctx:

    def __init__(self, nc, sems):
        self.nc = nc
        self.sems = sems
        self.eng = {}
        self.cur = None
        self.counts = None

    def start(self, name, handle):
        self.cur = name
        self.handle = handle
        self.counts = {k: 0 for k in self.sems}

    def dma(self, q, out, in_, sem, transpose=False, slow=False):
        self.counts[sem] += 16
        if self.cur == q:
            if transpose:
                self.handle.dma_start_transpose(out, in_).then_inc(self.sems[sem], 16)
            else:
                self.handle.dma_start(out=out, in_=in_,
                                      allow_slow_non_contiguous=slow).then_inc(
                    self.sems[sem], 16)
        return self.counts[sem]

    def op(self, ename, fn, sem=None, inc=1):
        if sem is not None:
            self.counts[sem] += inc
        if self.cur == ename:
            ins = fn(self.handle)
            if sem is not None and inc > 0:
                ins.then_inc(self.sems[sem], inc)
        return self.counts[sem] if sem is not None else None

    def enop(self, ename, n=2):
        if self.cur == ename:
            for _ in range(n):
                self.handle.engine_nop()

    def wait(self, ename, sem, val):
        if self.cur == ename:
            self.handle.wait_ge(self.sems[sem], val)

    def cc(self, ename, op, ins, outs, sem):
        self.counts[sem] += 1
        if self.cur == ename:
            self.handle.collective_compute(
                "AllReduce", op, replica_groups=[list(range(8))], ins=ins, outs=outs
            ).then_inc(self.sems[sem], 1)
        return self.counts[sem]


def bank_splits(a, b, align=512):
    out = []
    while a < b:
        e = min(b, (a // align + 1) * align)
        out.append((a, e - a))
        a = e
    return out


def build_nc(upto=None):
    nc = bass.Bass("TRN2", target_bir_lowering=False)

    x_in = nc.dram_tensor("x", [T, D], F32, kind="ExternalInput").ap()
    w_in = {}
    for nm, sh in (("Wq", [D, D]), ("Wk", [D, D]), ("Wv", [D, D]), ("Wm", [D, D]),
                   ("w_gate", [I, D]), ("w_up", [I, D]), ("w_down", [D, I])):
        w_in[nm] = nc.dram_tensor(nm, sh, F32, kind="ExternalInput").ap()
    b_in = {nm: nc.dram_tensor(nm, [D], F32, kind="ExternalInput").ap()
            for nm in ("bq", "bk", "bv", "bm")}
    ident_bf_in = nc.dram_tensor("ident_bf", [128, 128], BF16, kind="ExternalInput").ap()
    maskt_in = nc.dram_tensor("maskt", [128, 128], F32, kind="ExternalInput").ap()
    sel_in = nc.dram_tensor("selh", [16, H * HS], F32, kind="ExternalInput").ap()
    selp_in = nc.dram_tensor("selp", [16, 8 * 128], F32, kind="ExternalInput").ap()
    y_out = nc.dram_tensor("y", [T, D], F32, kind="ExternalOutput").ap()

    xs = nc.dram_tensor("xs", [T, D], F32).ap()
    w_bf = {}
    for nm, sh in (("Wq", [D, D]), ("Wk", [D, D]), ("Wv", [D, D]), ("Wm", [D, D]),
                   ("w_gate", [I, D]), ("w_up", [I, D]), ("w_down", [D, I])):
        w_bf[nm] = nc.dram_tensor(nm + "_bf", sh, BF16).ap()
    sp = {nm: nc.dram_tensor("sp_" + nm, [I, T], BF16).ap()
          for nm in ("up", "gate", "p2", "p3")}
    qsp = nc.dram_tensor("qsp", [D, S], BF16).ap()
    ksp = nc.dram_tensor("ksp", [D, S], BF16).ap()
    cc_i = nc.dram_tensor("cc_i", [1, 16], F32).ap()
    cc_o = nc.dram_tensor("cc_o", [1, 16], F32).ap()

    st = ExitStack()
    with st:
        sb = lambda n_, sh, dt: st.enter_context(nc.sbuf_tensor(n_, sh, dt))
        L = dict(x_in=x_in, w_in=w_in, b_in=b_in, y_out=y_out, xs=xs, w_bf=w_bf,
                 sp=sp, qsp=qsp, ksp=ksp, cc_i=cc_i, cc_o=cc_o,
                 ident_bf_in=ident_bf_in, maskt_in=maskt_in, sel_in=sel_in,
                 selp_in=selp_in, upto=upto)
        L["BB"] = sb("BB", [128, 65536], BF16)
        L["BF"] = sb("BFA", [128, 8192], F32)
        L["ID"] = sb("ID", [128, 128], BF16)
        L["MASK"] = sb("MASK", [128, 128], F32)
        L["SEL"] = sb("SEL", [16, H * HS], F32)
        L["SELP"] = sb("SELP", [16, 8 * 128], F32)
        L["ONES"] = sb("ONESF", [128, 128], F32)
        L["KCOL"] = sb("KCOL", [128, 1], F32)
        L["SC"] = sb("SCC", [128, 32], F32)
        L["BQC"] = sb("BQC", [128, 8], F32)
        L["BKC"] = sb("BKC", [128, 8], F32)
        L["BMC"] = sb("BMC", [128, 8], F32)
        L["BVR1"] = sb("BVR1", [1, D], F32)
        L["BVROW"] = sb("BVROW", [128, D], F32)
        L["STAT"] = sb("STAT", [128, 256], F32)
        L["UGC"] = sb("UGC", [128, 448], F32)
        L["CCIN"] = sb("CCIN", [1, 16], F32)
        L["CCOUT"] = sb("CCOUT", [1, 16], F32)
        L["SM"] = sb("SM", [128, 64], F32)
        L["INVL"] = sb("INVL", [16, S], F32)
        L["NEGM"] = sb("NEGM", [128, 16], F32)
        L["BCOL"] = sb("BCOL", [128, 1], BF16)
        L["PS_SC"] = st.enter_context(nc.psum_tensor("PS_SC", [128, 1024], F32))
        L["PS_PV"] = st.enter_context(nc.psum_tensor("PS_PV", [128, 2048], F32))
        L["PS_X"] = st.enter_context(nc.psum_tensor("PS_X", [128, 512], F32))
        L["PS_T"] = st.enter_context(nc.psum_tensor("PS_T", [128, 1024], BF16))

        sems = {}
        for s_ in ("dx", "dw", "dsp", "dst", "mm", "ev", "vv", "ss", "gg", "cc", "dq",
                   "fsy", "fte", "fve", "fsc", "fgp"):
            sems[s_] = st.enter_context(nc.semaphore("sem_" + s_))
        block = st.enter_context(nc.Block())
        ctx = Ctx(nc, sems)

        @block.sync
        def _(eng):
            ctx.start("sync", eng)
            emit_program(ctx, L)

        @block.tensor
        def _(eng):
            ctx.start("tensor", eng)
            emit_program(ctx, L)

        @block.vector
        def _(eng):
            ctx.start("vector", eng)
            emit_program(ctx, L)

        @block.scalar
        def _(eng):
            ctx.start("scalar", eng)
            emit_program(ctx, L)

        @block.gpsimd
        def _(eng):
            ctx.start("gpsimd", eng)
            emit_program(ctx, L)

    return nc


XQT = 0
WUG = 32768
STG = 57344
CD_IN = 0
CD_OUT = 8192
CD_UP = 16384
ORQ, WDT, PIN, FFO = 0, 24576, 49152, 53248
QKH = 0
VP = 8192
PRB = 24832
OT = 26624
LN2T = 43008
WMQ = 43008
APQ = 51200
WAT = 59392
AST = 63488

XF = 0
XF2 = 2048
YF = 4096


def emit_program(c, L):
    BB, BFA, SC, STAT, UGC, SM = L["BB"], L["BF"], L["SC"], L["STAT"], L["UGC"], L["SM"]
    PS_SC, PS_PV, PS_X, PS_T = L["PS_SC"], L["PS_PV"], L["PS_X"], L["PS_T"]
    KCOL, ONES, CCIN, CCOUT = L["KCOL"], L["ONES"], L["CCIN"], L["CCOUT"]
    upto = L["upto"]

    S0C, S0IC, SGC, SGIC, SG65C, SUIC, S2IC, S2UC, S3IC, S3HC = range(10)

    def sc(i):
        return SC[:, i:i + 1]

    def fence():
        mk = {"sync": "fsy", "tensor": "fte", "vector": "fve",
              "scalar": "fsc", "gpsimd": "fgp"}
        for e in ENGS:
            c.op(e, lambda x: x.nop(), sem=mk[e])
        vals = {e: c.counts[mk[e]] for e in ENGS}
        for e in ENGS:
            for o in ENGS:
                if o != e:
                    c.wait(e, mk[o], vals[o])

    c.dma("sync", L["ID"][:, :], L["ident_bf_in"], "dw")
    c.dma("sync", L["MASK"][:, :], L["maskt_in"], "dw")
    c.dma("sync", L["SEL"][:, :], L["sel_in"], "dw")
    c.dma("sync", L["SELP"][:, :], L["selp_in"], "dw")
    for nm, tgt in (("bq", L["BQC"]), ("bk", L["BKC"]), ("bm", L["BMC"])):
        c.dma("sync", tgt[:, :],
              bass.AP(L["b_in"][nm].tensor, 0, [[1, 128], [128, 8]]), "dw", slow=True)
    c.dma("sync", L["BVR1"][:, :],
          bass.AP(L["b_in"]["bv"].tensor, 0, [[1024, 1], [1, 1024]]), "dw")
    consts_dw = c.counts["dw"]

    c.op("vector", lambda v: v.memset(ONES[:, :], 1.0))
    c.enop("vector")
    c.op("vector", lambda v: v.tensor_scalar(SM[:, 0:1], ONES[:, 0:1], K52, None, OP.add))
    c.enop("vector")
    c.op("vector", lambda v: v.tensor_scalar(SM[:, 1:2], SM[:, 0:1], K52, None, OP.subtract))
    c.enop("vector")
    c.op("vector", lambda v: v.tensor_scalar(KCOL[:, :], SM[:, 1:2], K52, K23, OP.mult, OP.add))
    c.enop("vector")
    c.op("vector", lambda v: v.memset(CCIN[:, :], 0.0))
    c.op("vector", lambda v: v.memset(L["INVL"][:, :], 0.0), sem="vv")
    kblend_vv = c.counts["vv"]

    c.wait("tensor", "dw", consts_dw)
    c.wait("tensor", "vv", kblend_vv)
    for half in range(2):
        c.op("tensor", lambda t, h=half: t.matmul(
            PS_X[:, 0:512], ONES[0:1, :], L["BVR1"][0:1, h * 512:(h + 1) * 512],
            start=True, stop=True), sem="mm")
        mmv = c.counts["mm"]
        c.wait("scalar", "mm", mmv)
        c.op("scalar", lambda s_, h=half: s_.copy(
            L["BVROW"][:, h * 512:(h + 1) * 512], PS_X[:, 0:512]), sem="ss")
        c.wait("tensor", "ss", c.counts["ss"])
    bvrow_ss = c.counts["ss"]

    cast_guard = []
    ci = 0
    for nm in ("Wq", "Wk", "Wv", "Wm"):
        for ch in range(ND):
            buf = BB[:, WUG + (ci % 2) * 3072: WUG + (ci % 2) * 3072 + D]
            if ci >= 2:
                c.wait("gpsimd", "dst", cast_guard[ci - 2])
            c.dma("gpsimd", buf, L["w_in"][nm][ch * 128:(ch + 1) * 128, :], "dq")
            c.wait("sync", "dq", c.counts["dq"])
            c.dma("sync", L["w_bf"][nm][ch * 128:(ch + 1) * 128, :], buf, "dst")
            cast_guard.append(c.counts["dst"])
            ci += 1

    for nm, rows, cols in (("w_up", I, D), ("w_gate", I, D), ("w_down", D, I)):
        nb = cols // 32
        for ch in range(rows // 128):
            WF = BFA[:, 0:cols]
            YW = BFA[:, 4096:4096 + cols]
            c.wait("sync", "vv", c.counts["vv"])
            c.dma("sync", WF, L["w_in"][nm][ch * 128:(ch + 1) * 128, :], "dx")
            dxv = c.counts["dx"]
            c.wait("vector", "dx", dxv)
            c.op("vector", lambda v, WF=WF, nb=nb: v.tensor_reduce(
                UGC[:, 0:nb], WF.rearrange("p (a b) -> p a b", a=nb),
                AX.X, OP.max, apply_absolute_value=True), sem="vv")
            c.enop("vector")
            c.op("vector", lambda v, nb=nb: v.tensor_scalar(
                UGC[:, 96:96 + nb], UGC[:, 0:nb], 1.0 / 7.0, 1e-12, OP.mult, OP.max))
            c.enop("vector")
            c.op("vector", lambda v, nb=nb: v.reciprocal(UGC[:, 192:192 + nb],
                                                         UGC[:, 96:96 + nb]))
            c.enop("vector")
            c.op("vector", lambda v, nb=nb: v.tensor_tensor(
                UGC[:, 288:288 + nb], UGC[:, 96:96 + nb], UGC[:, 192:192 + nb],
                OP.mult))
            c.enop("vector")
            c.op("vector", lambda v, nb=nb: v.tensor_scalar(
                UGC[:, 288:288 + nb], UGC[:, 288:288 + nb], -1.0, 2.0, OP.mult, OP.add))
            c.enop("vector")
            c.op("vector", lambda v, nb=nb: v.tensor_tensor(
                UGC[:, 192:192 + nb], UGC[:, 192:192 + nb], UGC[:, 288:288 + nb],
                OP.mult))
            c.enop("vector")
            def bc0(col0, nb):
                a = UGC[:, col0:col0 + nb]
                return bass.AP(a.tensor, a.offset, [a.ap[0], [1, nb], [0, 32]])
            c.op("vector", lambda v, WF=WF, YW=YW, nb=nb: v.tensor_tensor(
                YW.rearrange("p (a b) -> p a b", a=nb),
                WF.rearrange("p (a b) -> p a b", a=nb), bc0(192, nb), OP.mult))
            c.op("vector", lambda v, WF=WF, YW=YW: v.tensor_scalar(
                WF, YW, KCOL[:, 0:1], None, OP.add))
            c.op("vector", lambda v, WF=WF, YW=YW: v.tensor_scalar(
                YW, WF, KCOL[:, 0:1], None, OP.subtract))
            stgw = BB[:, STG:STG + cols]
            c.op("vector", lambda v, YW=YW, stgw=stgw, nb=nb: v.tensor_tensor(
                stgw.rearrange("p (a b) -> p a b", a=nb),
                YW.rearrange("p (a b) -> p a b", a=nb), bc0(96, nb), OP.mult),
                sem="vv")
            vvw = c.counts["vv"]
            c.wait("sync", "vv", vvw)
            c.dma("sync", L["w_bf"][nm][ch * 128:(ch + 1) * 128, :], stgw, "dst")
    wcast_dst = c.counts["dst"]
    fence()

    def barrier():
        c.wait("sync", "vv", c.counts["vv"])
        c.wait("sync", "dst", c.counts["dst"])
        c.dma("sync", L["cc_i"], CCIN[:, :], "dst")
        c.wait("gpsimd", "dst", c.counts["dst"])
        c.cc("gpsimd", OP.max, [L["cc_i"]], [L["cc_o"]], "cc")
        c.wait("sync", "cc", c.counts["cc"])
        c.dma("sync", CCOUT[:, :], L["cc_o"], "dx")
        return c.counts["dx"]

    def bcast_scalar(src_ap, dst_slot):
        c.op("tensor", lambda t: t.matmul(PS_X[:, 0:1], ONES[0:1, :], src_ap,
                                          start=True, stop=True), sem="mm")
        c.wait("scalar", "mm", c.counts["mm"])
        c.op("scalar", lambda s_: s_.copy(sc(dst_slot), PS_X[:, 0:1]), sem="ss")
        c.wait("tensor", "ss", c.counts["ss"])

    def partition_absmax_to_ccin(colvec_ap, ccslot):
        c.op("vector", lambda v: v.tensor_copy(L["BCOL"][:, :], colvec_ap), sem="vv")
        c.wait("tensor", "vv", c.counts["vv"])
        c.op("tensor", lambda t: t.transpose(PS_T[0:1, 0:128], L["BCOL"][:, :],
                                             L["ID"][:, :]), sem="mm")
        c.wait("vector", "mm", c.counts["mm"])
        c.op("vector", lambda v: v.tensor_reduce(
            CCIN[0:1, ccslot:ccslot + 1], PS_T[0:1, 0:128], AX.X, OP.max), sem="vv")
        c.wait("tensor", "vv", c.counts["vv"])

    def ln_stats(src, soff, ntile, with_minmax):
        guards = []
        for j in range(ntile):
            jt = soff // 128 + j
            xb = BFA[:, XF + (j % 2) * 1024: XF + (j % 2) * 1024 + 1024]
            if j >= 2:
                gv, gs = guards[j - 2]
                c.wait("sync", "vv", gv)
                c.wait("sync", "ss", gs)
            c.dma("sync", xb, src[jt * 128:(jt + 1) * 128, :], "dx")
            dxv = c.counts["dx"]
            c.wait("vector", "dx", dxv)
            c.op("vector", lambda v, xb=xb, j=j: v.tensor_scalar(
                BFA[:, YF:YF + 1024], xb, 1.0, None, OP.mult, OP.add,
                accum_out=STAT[:, j:j + 1]), sem="vv")
            if with_minmax:
                c.op("vector", lambda v, xb=xb, j=j: v.tensor_reduce(
                    STAT[:, 64 + j:65 + j], xb, AX.X, OP.max), sem="vv")
                c.op("vector", lambda v, xb=xb, j=j: v.tensor_reduce(
                    STAT[:, 96 + j:97 + j], xb, AX.X, OP.min), sem="vv")
            c.wait("scalar", "dx", dxv)
            c.op("scalar", lambda s_, xb=xb, j=j: s_.activation(
                BFA[:, YF + 2048:YF + 3072], xb, AF.Square,
                accum_out=STAT[:, 32 + j:33 + j]), sem="ss")
            guards.append((c.counts["vv"], c.counts["ss"]))
        c.wait("vector", "vv", c.counts["vv"])
        c.wait("vector", "ss", c.counts["ss"])
        c.op("vector", lambda v: v.tensor_scalar(STAT[:, 128:128 + ntile],
                                                 STAT[:, 0:ntile], 1.0 / D, None, OP.mult))
        c.enop("vector")
        c.op("vector", lambda v: v.tensor_scalar(STAT[:, 160:160 + ntile],
                                                 STAT[:, 32:32 + ntile], 1.0 / D, None, OP.mult))
        c.enop("vector")
        c.op("vector", lambda v: v.scalar_tensor_tensor(
            SM[:, 32:32 + ntile], STAT[:, 128:128 + ntile], -1.0,
            STAT[:, 128:128 + ntile], OP.mult, OP.mult))
        c.enop("vector")
        c.op("vector", lambda v: v.tensor_tensor(STAT[:, 160:160 + ntile],
                                                 STAT[:, 160:160 + ntile],
                                                 SM[:, 32:32 + ntile], OP.add))
        c.enop("vector")
        c.op("vector", lambda v: v.tensor_scalar(STAT[:, 160:160 + ntile],
                                                 STAT[:, 160:160 + ntile], EPS, None, OP.add))
        c.enop("vector")
        c.op("vector", lambda v: v.reciprocal(STAT[:, 160:160 + ntile],
                                              STAT[:, 160:160 + ntile]), sem="vv")
        c.wait("scalar", "vv", c.counts["vv"])
        c.op("scalar", lambda s_: s_.activation(STAT[:, 160:160 + ntile],
                                                STAT[:, 160:160 + ntile], AF.Sqrt), sem="ss")
        return c.counts["ss"]

    def ffn(call, src, dst, stop_at=None):
        def _stop_copy():
            for j in range(NT):
                c.dma("sync", dst[j * 128:(j + 1) * 128, :],
                      src[j * 128:(j + 1) * 128, :], "dst")
        base = 0 if call == 0 else 5
        fence()
        c.wait("sync", "dst", c.counts["dst"])
        rstd_ss = ln_stats(src, 0, 32, True)
        c.wait("vector", "ss", rstd_ss)
        c.op("vector", lambda v: v.tensor_tensor(SM[:, 0:32], STAT[:, 64:96],
                                                 STAT[:, 128:160], OP.subtract))
        c.enop("vector")
        c.op("vector", lambda v: v.tensor_tensor(SM[:, 32:64], STAT[:, 128:160],
                                                 STAT[:, 96:128], OP.subtract))
        c.enop("vector")
        c.op("vector", lambda v: v.tensor_tensor(SM[:, 0:32], SM[:, 0:32],
                                                 SM[:, 32:64], OP.max))
        c.enop("vector")
        c.op("vector", lambda v: v.tensor_tensor(SM[:, 0:32], SM[:, 0:32],
                                                 STAT[:, 160:192], OP.mult))
        c.enop("vector")
        c.op("vector", lambda v: v.tensor_reduce(SM[:, 40:41], SM[:, 0:32],
                                                 AX.X, OP.max), sem="vv")
        c.enop("vector")
        partition_absmax_to_ccin(SM[:, 40:41], base + 0)
        bval = barrier()

        c.wait("vector", "dx", bval)
        c.op("vector", lambda v, b=base: v.tensor_scalar(
            SM[0:1, 0:1], CCOUT[0:1, b:b + 1], 1.0 / 32767.0, 1e-12,
            OP.mult, OP.max))
        c.enop("vector")
        c.op("vector", lambda v: v.reciprocal(SM[0:1, 1:2], SM[0:1, 0:1]), sem="vv")
        c.enop("vector")
        c.wait("tensor", "vv", c.counts["vv"])
        bcast_scalar(SM[0:1, 0:1], S0C)
        bcast_scalar(SM[0:1, 1:2], S0IC)
        if stop_at == "A":
            _stop_copy(); return
        c.wait("vector", "ss", c.counts["ss"])
        c.op("vector", lambda v: v.tensor_scalar(STAT[:, 192:224], STAT[:, 160:192],
                                                 sc(S0IC), None, OP.mult))
        c.enop("vector")
        c.op("vector", lambda v: v.scalar_tensor_tensor(
            STAT[:, 224:256], STAT[:, 128:160], -1.0, STAT[:, 192:224],
            OP.mult, OP.mult), sem="vv")
        scl_vv = c.counts["vv"]

        c.wait("scalar", "vv", scl_vv)
        xq_guard = []
        for j in range(NT):
            xb = BFA[:, XF + (j % 2) * 1024: XF + (j % 2) * 1024 + 1024]
            if j >= 2:
                c.wait("sync", "ss", xq_guard[j - 2])
            c.dma("sync", xb, src[j * 128:(j + 1) * 128, :], "dx")
            dxv = c.counts["dx"]
            c.wait("scalar", "dx", dxv)
            c.op("scalar", lambda s_, xb=xb, j=j: s_.activation(
                BFA[:, YF:YF + 1024], xb, AF.Identity,
                bias=STAT[:, 224 + j:225 + j], scale=STAT[:, 192 + j:193 + j]), sem="ss")
            sv = c.counts["ss"]
            c.wait("vector", "ss", sv)
            c.op("vector", lambda v: v.tensor_scalar(
                BFA[:, YF + 1024:YF + 2048], BFA[:, YF:YF + 1024], KCOL[:, 0:1],
                None, OP.add))
            c.op("vector", lambda v: v.tensor_scalar(
                BB[:, STG:STG + 1024], BFA[:, YF + 1024:YF + 2048], KCOL[:, 0:1],
                None, OP.subtract), sem="vv")
            vv0 = c.counts["vv"]
            c.wait("tensor", "vv", vv0)
            for dc in range(ND):
                c.op("tensor", lambda t, dc=dc: t.transpose(
                    PS_T[:, dc * 128:(dc + 1) * 128],
                    BB[:, STG + dc * 128: STG + (dc + 1) * 128], L["ID"][:, :]),
                    sem="mm", inc=(1 if dc == ND - 1 else 0))
            mmv = c.counts["mm"]
            c.wait("scalar", "mm", mmv)
            out3 = BB[:, XQT:XQT + 32768].rearrange("p (a b) -> p a b", a=8)[
                :, :, j * 128:(j + 1) * 128]
            c.op("scalar", lambda s_, out3=out3: s_.copy(
                out3, PS_T[:, :].rearrange("p (a b) -> p a b", a=8)), sem="ss")
            xq_guard.append(c.counts["ss"])
            c.wait("tensor", "ss", c.counts["ss"])
        xq_ss = c.counts["ss"]

        for wi, wname in enumerate(("w_up", "w_gate")):
            for dc in range(ND):
                c.dma("sync", BB[:, WUG + dc * I: WUG + (dc + 1) * I],
                      L["w_bf"][wname][:, dc * 128:(dc + 1) * 128], "dw", transpose=True)
            dwv = c.counts["dw"]
            c.wait("sync", "dw", dwv)
            c.wait("tensor", "dw", dwv)
            c.wait("tensor", "ss", xq_ss)
            grp_ev, grp_dst, grp_vv = [], [], []
            for g in range(NI * 8):
                ic, tb = divmod(g, 8)
                half = g % 2
                pcol = PS_SC[:, half * 512:(half + 1) * 512]
                stg = BB[:, STG + half * 512: STG + half * 512 + 512]
                if g >= 2:
                    c.wait("tensor", "ev", grp_ev[g - 2])
                for dc in range(ND):
                    c.op("tensor", lambda t, ic=ic, tb=tb, dc=dc, pcol=pcol: t.matmul(
                        pcol,
                        BB[:, WUG + dc * I + ic * 128: WUG + dc * I + (ic + 1) * 128],
                        BB[:, XQT + dc * 4096 + tb * 512: XQT + dc * 4096 + (tb + 1) * 512],
                        start=(dc == 0), stop=(dc == ND - 1)),
                        sem="mm", inc=(1 if dc == ND - 1 else 0))
                mmv = c.counts["mm"]
                c.wait("scalar", "mm", mmv)
                if g >= 2:
                    c.wait("scalar", "dst", grp_dst[g - 2])
                    c.wait("scalar", "vv", grp_vv[g - 2])
                c.op("scalar", lambda s_, pcol=pcol, stg=stg: s_.activation(
                    stg, pcol, AF.Identity, scale=sc(S0C)), sem="ev")
                evv = c.counts["ev"]
                grp_ev.append(evv)
                c.wait("vector", "ev", evv)
                cidx = wi * 192 + g
                c.op("vector", lambda v, stg=stg, cidx=cidx: v.tensor_reduce(
                    UGC[:, cidx:cidx + 1], stg, AX.X, OP.max,
                    apply_absolute_value=True), sem="vv")
                grp_vv.append(c.counts["vv"])
                c.wait("sync", "ev", evv)
                c.dma("sync", L["sp"][{"w_up": "up", "w_gate": "gate"}[wname]][
                    ic * 128:(ic + 1) * 128, tb * 512:(tb + 1) * 512], stg, "dst")
                grp_dst.append(c.counts["dst"])
        c.wait("vector", "vv", c.counts["vv"])
        c.op("vector", lambda v: v.tensor_reduce(SM[:, 41:42], UGC[:, 0:192],
                                                 AX.X, OP.max), sem="vv")
        c.enop("vector")
        c.op("vector", lambda v: v.tensor_reduce(SM[:, 42:43], UGC[:, 192:384],
                                                 AX.X, OP.max), sem="vv")
        c.enop("vector")
        partition_absmax_to_ccin(SM[:, 41:42], base + 1)
        partition_absmax_to_ccin(SM[:, 42:43], base + 2)
        bval = barrier()

        c.wait("vector", "dx", bval)
        c.op("vector", lambda v, b=base: v.tensor_scalar(
            SM[0:1, 2:3], CCOUT[0:1, b + 1:b + 2], 1.0 / 32767.0, 1e-12, OP.mult, OP.max))
        c.enop("vector")
        c.op("vector", lambda v, b=base: v.tensor_scalar(
            SM[0:1, 3:4], CCOUT[0:1, b + 2:b + 3], 1.0 / 32767.0, 1e-12, OP.mult, OP.max))
        c.enop("vector")
        c.op("vector", lambda v: v.reciprocal(SM[0:1, 4:5], SM[0:1, 2:3]))
        c.enop("vector")
        c.op("vector", lambda v: v.reciprocal(SM[0:1, 5:6], SM[0:1, 3:4]))
        c.enop("vector")
        c.op("vector", lambda v: v.tensor_scalar(SM[0:1, 6:7], SM[0:1, 3:4],
                                                 1.0 / 65536.0, None, OP.mult), sem="vv")
        c.wait("tensor", "vv", c.counts["vv"])
        bcast_scalar(SM[0:1, 3:4], SGC)
        bcast_scalar(SM[0:1, 5:6], SGIC)
        bcast_scalar(SM[0:1, 6:7], SG65C)
        bcast_scalar(SM[0:1, 4:5], SUIC)
        if stop_at == "B":
            _stop_copy(); return

        cd_guard = []
        for ti in range(NI * 2):
            ic, th = divmod(ti, 2)
            gin = BB[:, CD_IN + (ti % 2) * 2048: CD_IN + (ti % 2) * 2048 + 2048]
            pout = BB[:, CD_OUT + (ti % 2) * 2048: CD_OUT + (ti % 2) * 2048 + 2048]
            if ti >= 2:
                gdst, ggv = cd_guard[ti - 2]
                c.wait("sync", "dst", gdst)
            c.dma("sync", gin, L["sp"]["gate"][ic * 128:(ic + 1) * 128,
                                               th * 2048:(th + 1) * 2048], "dsp")
            dv = c.counts["dsp"]
            Y = BFA[:, 0:2048]
            RG = BFA[:, 2048:4096]
            SGm = BFA[:, 4096:6144]
            Y2 = BFA[:, 6144:8192]
            c.wait("scalar", "dsp", dv)
            c.op("scalar", lambda s_, gin=gin, Y=Y: s_.activation(
                Y, gin, AF.Identity, bias=KCOL[:, 0:1], scale=sc(SGIC)), sem="ss")
            sv = c.counts["ss"]
            c.wait("vector", "ss", sv)
            c.op("vector", lambda v, Y=Y, RG=RG: v.tensor_scalar(
                RG, Y, KCOL[:, 0:1], None, OP.subtract), sem="vv")
            vv1 = c.counts["vv"]
            c.wait("scalar", "vv", vv1)
            c.op("scalar", lambda s_, RG=RG, SGm=SGm: s_.activation(
                SGm, RG, AF.Sigmoid, scale=sc(SGC)), sem="ss")
            sv2 = c.counts["ss"]
            c.wait("vector", "ss", sv2)
            c.op("vector", lambda v, SGm=SGm, Y2=Y2: v.tensor_scalar(
                Y2, SGm, 65536.0, KCOL[:, 0:1], OP.mult, OP.add))
            c.op("vector", lambda v, Y2=Y2: v.tensor_scalar(
                Y2, Y2, KCOL[:, 0:1], 65535.0, OP.subtract, OP.min), sem="vv")
            vv2 = c.counts["vv"]
            c.op("vector", lambda v, RG=RG, Y2=Y2, pout=pout: v.scalar_tensor_tensor(
                pout, RG, sc(SG65C), Y2, OP.mult, OP.mult), sem="gg")
            gv = c.counts["gg"]
            c.wait("vector", "gg", gv)
            c.op("vector", lambda v, pout=pout, ti=ti: v.tensor_reduce(
                UGC[:, 384 + ti:385 + ti], pout, AX.X, OP.max,
                apply_absolute_value=True), sem="vv")
            c.wait("sync", "gg", gv)
            c.dma("sync", L["sp"]["p2"][ic * 128:(ic + 1) * 128,
                                        th * 2048:(th + 1) * 2048], pout, "dst")
            cd_guard.append((c.counts["dst"], c.counts["vv"]))
        c.wait("vector", "vv", c.counts["vv"])
        c.op("vector", lambda v: v.tensor_reduce(SM[:, 43:44], UGC[:, 384:432],
                                                 AX.X, OP.max), sem="vv")
        c.enop("vector")
        partition_absmax_to_ccin(SM[:, 43:44], base + 3)
        bval = barrier()

        c.wait("vector", "dx", bval)
        c.op("vector", lambda v, b=base: v.tensor_scalar(
            SM[0:1, 7:8], CCOUT[0:1, b + 3:b + 4], 1.0 / 32767.0, 1e-12, OP.mult, OP.max))
        c.enop("vector")
        c.op("vector", lambda v: v.reciprocal(SM[0:1, 8:9], SM[0:1, 7:8]))
        c.enop("vector")
        c.op("vector", lambda v: v.tensor_tensor(SM[0:1, 9:10], SM[0:1, 7:8],
                                                 SM[0:1, 2:3], OP.mult), sem="vv")
        c.wait("tensor", "vv", c.counts["vv"])
        bcast_scalar(SM[0:1, 8:9], S2IC)
        bcast_scalar(SM[0:1, 9:10], S2UC)
        if stop_at == "C":
            _stop_copy(); return

        cd_guard = []
        d_prev_vv = 0
        for ti in range(NI * 2):
            ic, th = divmod(ti, 2)
            pin2 = BB[:, CD_IN + (ti % 2) * 2048: CD_IN + (ti % 2) * 2048 + 2048]
            uin = BB[:, CD_UP + (ti % 2) * 2048: CD_UP + (ti % 2) * 2048 + 2048]
            pout = BB[:, CD_OUT + (ti % 2) * 2048: CD_OUT + (ti % 2) * 2048 + 2048]
            if ti >= 2:
                gdst, ggv = cd_guard[ti - 2]
                c.wait("sync", "dst", gdst)
            c.dma("sync", pin2, L["sp"]["p2"][ic * 128:(ic + 1) * 128,
                                              th * 2048:(th + 1) * 2048], "dsp")
            c.dma("sync", uin, L["sp"]["up"][ic * 128:(ic + 1) * 128,
                                             th * 2048:(th + 1) * 2048], "dsp")
            dv = c.counts["dsp"]
            Y = BFA[:, 0:2048]
            G2 = BFA[:, 2048:4096]
            Y4 = BFA[:, 4096:6144]
            UQ = BFA[:, 6144:8192]
            c.wait("scalar", "dsp", dv)
            if ti >= 1:
                c.wait("scalar", "vv", d_prev_vv)
            c.op("scalar", lambda s_, pin2=pin2, Y=Y: s_.activation(
                Y, pin2, AF.Identity, bias=KCOL[:, 0:1], scale=sc(S2IC)), sem="ss")
            sv = c.counts["ss"]
            c.op("scalar", lambda s_, uin=uin, Y4=Y4: s_.activation(
                Y4, uin, AF.Identity, bias=KCOL[:, 0:1], scale=sc(SUIC)), sem="ss")
            sv2 = c.counts["ss"]
            c.wait("vector", "ss", sv2)
            c.op("vector", lambda v, Y=Y, G2=G2: v.tensor_scalar(
                G2, Y, KCOL[:, 0:1], None, OP.subtract))
            c.op("vector", lambda v, Y4=Y4, UQ=UQ: v.tensor_scalar(
                UQ, Y4, KCOL[:, 0:1], None, OP.subtract), sem="vv")
            vv1 = c.counts["vv"]
            d_prev_vv = vv1
            c.op("vector", lambda v, G2=G2, UQ=UQ, pout=pout: v.scalar_tensor_tensor(
                pout, G2, sc(S2UC), UQ, OP.mult, OP.mult), sem="gg")
            gv = c.counts["gg"]
            c.wait("vector", "gg", gv)
            c.op("vector", lambda v, pout=pout, ti=ti: v.tensor_reduce(
                UGC[:, 384 + ti:385 + ti], pout, AX.X, OP.max,
                apply_absolute_value=True), sem="vv")
            c.wait("sync", "gg", gv)
            c.dma("sync", L["sp"]["p3"][ic * 128:(ic + 1) * 128,
                                        th * 2048:(th + 1) * 2048], pout, "dst")
            cd_guard.append((c.counts["dst"], c.counts["vv"]))
        c.wait("vector", "vv", c.counts["vv"])
        c.op("vector", lambda v: v.tensor_reduce(SM[:, 44:45], UGC[:, 384:432],
                                                 AX.X, OP.max), sem="vv")
        c.enop("vector")
        partition_absmax_to_ccin(SM[:, 44:45], base + 4)
        bval = barrier()

        c.wait("vector", "dx", bval)
        c.op("vector", lambda v, b=base: v.tensor_scalar(
            SM[0:1, 10:11], CCOUT[0:1, b + 4:b + 5], 1.0 / 32767.0, 1e-12, OP.mult, OP.max))
        c.enop("vector")
        c.op("vector", lambda v: v.reciprocal(SM[0:1, 11:12], SM[0:1, 10:11]))
        c.enop("vector")
        c.op("vector", lambda v: v.tensor_scalar(SM[0:1, 12:13], SM[0:1, 10:11],
                                                 0.5, None, OP.mult), sem="vv")
        c.wait("tensor", "vv", c.counts["vv"])
        bcast_scalar(SM[0:1, 11:12], S3IC)
        bcast_scalar(SM[0:1, 12:13], S3HC)
        if stop_at == "D":
            _stop_copy(); return

        stopE = stop_at if (stop_at or "").startswith("E") else None
        if stopE != "EY":
            for ic in range(NI):
                c.dma("sync", BB[:, WDT + ic * 1024: WDT + (ic + 1) * 1024],
                      L["w_bf"]["w_down"][:, ic * 128:(ic + 1) * 128], "dw", transpose=True)
            dwv = c.counts["dw"]
            c.wait("sync", "dw", dwv)
        else:
            dwv = c.counts["dw"]
        prev_down_mm = None
        for q in (range(4) if stopE not in ("EZ", "EY") else []):
            or_guard = []
            if prev_down_mm is not None:
                c.wait("vector", "mm", prev_down_mm)
            for ic in range(NI):
                pin3 = BB[:, PIN + (ic % 2) * 1024: PIN + (ic % 2) * 1024 + 1024]
                if ic >= 2:
                    c.wait("sync", "vv", or_guard[ic - 2])
                c.dma("sync", pin3, L["sp"]["p3"][ic * 128:(ic + 1) * 128,
                                                  q * 1024:(q + 1) * 1024], "dsp")
                dv = c.counts["dsp"]
                Y = BFA[:, YF + (ic % 2) * 1024: YF + (ic % 2) * 1024 + 1024]
                c.wait("scalar", "dsp", dv)
                if ic >= 2:
                    c.wait("scalar", "vv", or_guard[ic - 2])
                c.op("scalar", lambda s_, pin3=pin3, Y=Y: s_.activation(
                    Y, pin3, AF.Identity, bias=KCOL[:, 0:1], scale=sc(S3IC)), sem="ss")
                sv = c.counts["ss"]
                c.wait("vector", "ss", sv)
                c.op("vector", lambda v, Y=Y, ic=ic: v.tensor_scalar(
                    BB[:, ORQ + ic * 1024: ORQ + (ic + 1) * 1024], Y, KCOL[:, 0:1],
                    None, OP.subtract), sem="vv")
                or_guard.append(c.counts["vv"])
            orq_vv = c.counts["vv"]
            if stopE == "E0":
                continue
            c.wait("tensor", "vv", orq_vv)
            if q == 0:
                c.wait("tensor", "dw", dwv)
            dn_ev = []
            for g in range(16):
                hc, tq = divmod(g, 2)
                half = g % 2
                pcol = PS_SC[:, half * 512:(half + 1) * 512]
                if g >= 2:
                    c.wait("tensor", "ev", dn_ev[g - 2])
                for ic in range(NI):
                    c.op("tensor", lambda t, hc=hc, tq=tq, ic=ic, pcol=pcol: t.matmul(
                        pcol,
                        BB[:, WDT + ic * 1024 + hc * 128: WDT + ic * 1024 + (hc + 1) * 128],
                        BB[:, ORQ + ic * 1024 + tq * 512: ORQ + ic * 1024 + (tq + 1) * 512],
                        start=(ic == 0), stop=(ic == NI - 1)),
                        sem="mm", inc=(1 if ic == NI - 1 else 0))
                mmv = c.counts["mm"]
                stg = BB[:, FFO + tq * 4096 + hc * 512: FFO + tq * 4096 + (hc + 1) * 512]
                c.wait("scalar", "mm", mmv)
                c.op("scalar", lambda s_, pcol=pcol, stg=stg: s_.activation(
                    stg, pcol, AF.Identity, scale=sc(S3HC)), sem="ev")
                dn_ev.append(c.counts["ev"])
            prev_down_mm = c.counts["mm"]
            ffo_ev = c.counts["ev"]
            if stopE == "E1":
                continue
            c.wait("tensor", "ev", ffo_ev)
            for jj in range(8):
                j = q * 8 + jj
                tq, sub = divmod(jj, 4)
                xb = BFA[:, XF + (jj % 2) * 1024: XF + (jj % 2) * 1024 + 1024]
                if jj >= 2 or q > 0:
                    c.wait("sync", "dst", c.counts["dst"])
                c.dma("sync", xb, src[j * 128:(j + 1) * 128, :], "dx")
                dxv = c.counts["dx"]
                for hc in range(ND):
                    c.op("tensor", lambda t, hc=hc, tq=tq, sub=sub: t.transpose(
                        PS_T[:, hc * 128:(hc + 1) * 128],
                        BB[:, FFO + tq * 4096 + hc * 512 + sub * 128:
                           FFO + tq * 4096 + hc * 512 + sub * 128 + 128],
                        L["ID"][:, :]), sem="mm", inc=(1 if hc == ND - 1 else 0))
                mmv = c.counts["mm"]
                c.wait("vector", "mm", mmv)
                c.wait("vector", "dx", dxv)
                c.op("vector", lambda v, xb=xb: v.tensor_tensor(
                    xb, xb, PS_T[:, :], OP.add), sem="vv")
                vvv = c.counts["vv"]
                c.wait("tensor", "vv", vvv)
                c.wait("sync", "vv", vvv)
                c.dma("sync", dst[j * 128:(j + 1) * 128, :], xb, "dst")
        if stopE in ("E0", "E1", "EZ", "EY"):
            _stop_copy()

    def attn(dst, dump_ot=False):
        fence()
        c.wait("sync", "dst", c.counts["dst"])
        for s in range(1 if dump_ot else NSEQ):
            soff = s * S
            rstd_ss = ln_stats(L["xs"], soff, 16, False)
            c.wait("vector", "ss", rstd_ss)
            c.op("vector", lambda v: v.scalar_tensor_tensor(
                STAT[:, 224:240], STAT[:, 128:144], -1.0, STAT[:, 160:176],
                OP.mult, OP.mult), sem="vv")
            bia_vv = c.counts["vv"]
            c.wait("scalar", "vv", bia_vv)
            ln_guard = []
            for j in range(16):
                jt = soff // 128 + j
                xb = BFA[:, XF2 + (j % 2) * 1024: XF2 + (j % 2) * 1024 + 1024]
                if j >= 2:
                    c.wait("sync", "ss", ln_guard[j - 2])
                c.dma("sync", xb, L["xs"][jt * 128:(jt + 1) * 128, :], "dx")
                dxv = c.counts["dx"]
                c.wait("scalar", "dx", dxv)
                c.op("scalar", lambda s_, xb=xb, j=j: s_.activation(
                    BB[:, AST:AST + 1024], xb, AF.Identity,
                    bias=STAT[:, 224 + j:225 + j], scale=STAT[:, 160 + j:161 + j]), sem="ss")
                sv = c.counts["ss"]
                c.wait("tensor", "ss", sv)
                for dc in range(ND):
                    c.op("tensor", lambda t, dc=dc: t.transpose(
                        PS_T[:, dc * 128:(dc + 1) * 128],
                        BB[:, AST + dc * 128: AST + (dc + 1) * 128], L["ID"][:, :]),
                        sem="mm", inc=(1 if dc == ND - 1 else 0))
                mmv = c.counts["mm"]
                c.wait("scalar", "mm", mmv)
                out3 = BB[:, LN2T:LN2T + 16384].rearrange("p (a b) -> p a b", a=8)[
                    :, :, j * 128:(j + 1) * 128]
                c.op("scalar", lambda s_, out3=out3: s_.copy(
                    out3, PS_T[:, :].rearrange("p (a b) -> p a b", a=8)), sem="ss")
                ln_guard.append(c.counts["ss"])
                c.wait("tensor", "ss", c.counts["ss"])
            ln2t_ss = c.counts["ss"]

            for wi, (wname, spt, bias_t) in enumerate(
                    (("Wq", L["qsp"], L["BQC"]), ("Wk", L["ksp"], L["BKC"]))):
                for whalf in range(2):
                    c.wait("sync", "mm", c.counts["mm"])
                    for dc in range(ND):
                        c.dma("sync", BB[:, WAT + dc * 512: WAT + (dc + 1) * 512],
                              L["w_bf"][wname][dc * 128:(dc + 1) * 128,
                                               whalf * 512:(whalf + 1) * 512], "dw")
                    dwv = c.counts["dw"]
                    c.wait("tensor", "dw", dwv)
                    c.wait("tensor", "ss", ln2t_ss)
                    g_ev, g_dst, g_vv = [], [], []
                    for g in range(16):
                        oc4, qb = divmod(g, 4)
                        oc = whalf * 4 + oc4
                        half = g % 2
                        pcol = PS_SC[:, half * 512:(half + 1) * 512]
                        stg = BB[:, AST + half * 512: AST + half * 512 + 512]
                        if g >= 2:
                            c.wait("tensor", "ev", g_ev[g - 2])
                        for dc in range(ND):
                            c.op("tensor", lambda t, oc4=oc4, qb=qb, dc=dc, pcol=pcol: t.matmul(
                                pcol,
                                BB[:, WAT + dc * 512 + oc4 * 128:
                                   WAT + dc * 512 + (oc4 + 1) * 128],
                                BB[:, LN2T + dc * 2048 + qb * 512:
                                   LN2T + dc * 2048 + (qb + 1) * 512],
                                start=(dc == 0), stop=(dc == ND - 1)),
                                sem="mm", inc=(1 if dc == ND - 1 else 0))
                        mmv = c.counts["mm"]
                        c.wait("scalar", "mm", mmv)
                        if g >= 2:
                            c.wait("scalar", "dst", g_dst[g - 2])
                            c.wait("scalar", "vv", g_vv[g - 2])
                        c.op("scalar", lambda s_, pcol=pcol, stg=stg, oc=oc, bias_t=bias_t:
                             s_.activation(stg, pcol, AF.Identity,
                                           bias=bias_t[:, oc:oc + 1]), sem="ev")
                        evv = c.counts["ev"]
                        g_ev.append(evv)
                        c.wait("vector", "ev", evv)
                        cidx = wi * 32 + oc * 4 + qb
                        c.op("vector", lambda v, stg=stg, cidx=cidx: v.tensor_reduce(
                            UGC[:, cidx:cidx + 1], stg, AX.X, OP.max,
                            apply_absolute_value=True), sem="vv")
                        g_vv.append(c.counts["vv"])
                        c.wait("sync", "ev", evv)
                        c.dma("sync", spt[oc * 128:(oc + 1) * 128,
                                          qb * 512:(qb + 1) * 512], stg, "dst")
                        g_dst.append(c.counts["dst"])
            qk_dst = c.counts["dst"]
            qk_vv = c.counts["vv"]

            for whalf in range(2):
                c.wait("sync", "mm", c.counts["mm"])
                for dc in range(ND):
                    c.dma("sync", BB[:, WAT + dc * 512: WAT + (dc + 1) * 512],
                          L["w_bf"]["Wv"][dc * 128:(dc + 1) * 128,
                                          whalf * 512:(whalf + 1) * 512], "dw")
                dwv = c.counts["dw"]
                c.wait("tensor", "dw", dwv)
                c.wait("tensor", "ss", ln2t_ss)
                v_ev = []
                for tt in range(16):
                    half = tt % 2
                    pcol = PS_SC[:, half * 512:(half + 1) * 512]
                    if tt >= 2:
                        c.wait("tensor", "ev", v_ev[tt - 2])
                    for dc in range(ND):
                        c.op("tensor", lambda t, tt=tt, dc=dc, pcol=pcol: t.matmul(
                            pcol,
                            BB[:, LN2T + dc * 2048 + tt * 128:
                               LN2T + dc * 2048 + (tt + 1) * 128],
                            BB[:, WAT + dc * 512: WAT + (dc + 1) * 512],
                            start=(dc == 0), stop=(dc == ND - 1)),
                            sem="mm", inc=(1 if dc == ND - 1 else 0))
                    mmv = c.counts["mm"]
                    c.wait("vector", "mm", mmv)
                    vp3 = BB[:, VP + tt * 1040: VP + tt * 1040 + 1040].rearrange(
                        "p (a b) -> p a b", a=16)[:, whalf * 8:(whalf + 1) * 8, 0:64]
                    c.op("vector", lambda v, vp3=vp3, pcol=pcol, whalf=whalf: v.tensor_tensor(
                        vp3, pcol.rearrange("p (a b) -> p a b", a=8),
                        L["BVROW"][:, whalf * 512:(whalf + 1) * 512].rearrange(
                            "p (a b) -> p a b", a=8), OP.add), sem="ev")
                    v_ev.append(c.counts["ev"])
            vdone_ev = c.counts["ev"]
            c.wait("vector", "ev", vdone_ev)
            ones3 = BB[:, VP:VP + 16640].rearrange("p (a b) -> p a b", a=256)[:, :, 64:65]
            c.op("vector", lambda v, ones3=ones3: v.memset(ones3, 1.0), sem="vv")
            vp_vv = c.counts["vv"]

            c.wait("vector", "vv", vp_vv)
            c.op("vector", lambda v: v.tensor_reduce(
                SM[:, 0:8], UGC[:, 0:32].rearrange("p (a b) -> p a b", a=8),
                AX.X, OP.max), sem="vv")
            c.enop("vector")
            c.op("vector", lambda v: v.tensor_reduce(
                SM[:, 8:16], UGC[:, 32:64].rearrange("p (a b) -> p a b", a=8),
                AX.X, OP.max), sem="vv")
            c.enop("vector")
            c.op("vector", lambda v: v.tensor_copy(BB[:, AST:AST + 16], SM[:, 0:16]),
                 sem="vv")
            c.wait("tensor", "vv", c.counts["vv"])
            c.op("tensor", lambda t: t.transpose(PS_T[0:16, 0:128], BB[:, AST:AST + 16],
                                                 L["ID"][:, :]), sem="mm")
            c.wait("vector", "mm", c.counts["mm"])
            c.op("vector", lambda v: v.tensor_reduce(
                SM[0:16, 16:17], PS_T[0:16, 0:64], AX.X, OP.max), sem="vv")
            c.enop("vector")
            c.op("vector", lambda v: v.tensor_reduce(
                SM[0:16, 17:18], PS_T[0:16, 64:128], AX.X, OP.max), sem="vv")
            mx_vv = c.counts["vv"]
            c.wait("tensor", "vv", mx_vv)
            c.wait("sync", "vv", mx_vv)
            c.dma("sync", SM[0:8, 20:22], SM[8:16, 16:18], "dx")
            dxv = c.counts["dx"]
            c.wait("vector", "dx", dxv)
            c.op("vector", lambda v: v.scalar_tensor_tensor(
                SM[0:8, 22:24], SM[0:8, 16:18], -8.0, SM[0:8, 20:22],
                OP.mult, OP.mult))
            c.enop("vector")
            c.op("vector", lambda v: v.tensor_scalar(
                SM[0:8, 22:24], SM[0:8, 22:24], 45.0, None, OP.add), sem="vv")
            negm_vv = c.counts["vv"]
            c.wait("sync", "vv", negm_vv)
            c.dma("sync", SM[0:1, 48:64], SM[0:8, 22:24], "dx")
            dxv = c.counts["dx"]
            c.wait("tensor", "dx", dxv)
            c.op("tensor", lambda t: t.matmul(PS_X[:, 0:16], ONES[0:1, :],
                                              SM[0:1, 48:64], start=True, stop=True),
                 sem="mm")
            c.wait("scalar", "mm", c.counts["mm"])
            c.op("scalar", lambda s_: s_.copy(L["NEGM"][:, :], PS_X[:, 0:16]), sem="ss")
            negm_ss = c.counts["ss"]
            c.wait("tensor", "ss", negm_ss)

            c.wait("sync", "dst", qk_dst)
            head_mm_end = []
            for h in range(H):
                qb_buf = BB[:, QKH + (h % 2) * 4096: QKH + (h % 2) * 4096 + 2048]
                kb_buf = BB[:, QKH + (h % 2) * 4096 + 2048: QKH + (h % 2) * 4096 + 4096]
                if h >= 2:
                    c.wait("sync", "mm", head_mm_end[h - 2])
                c.dma("sync", qb_buf[0:64, :], L["qsp"][h * 64:(h + 1) * 64, :], "dsp")
                c.dma("sync", kb_buf[0:64, :], L["ksp"][h * 64:(h + 1) * 64, :], "dsp")
                dqk = c.counts["dsp"]
                c.wait("tensor", "dsp", dqk)
                c.wait("tensor", "ss", negm_ss)
                wins = []
                for kc in range(16):
                    q0 = kc * 128
                    for w in range(q0 // 512, 4):
                        a = max(q0, w * 512)
                        b = (w + 1) * 512
                        wins.append((kc, a, b, a == q0))
                exp_ss = {}
                pv_done_mm = {}
                for widx, (kc, a, b, diag) in enumerate(wins):
                    half = widx % 2
                    pcol = PS_SC[:, half * 512: half * 512 + (b - a)]
                    prbo = PRB + half * 512
                    if widx >= 2:
                        c.wait("tensor", "ss", exp_ss[widx - 2])
                    pieces = ([(0, 128)] + bank_splits(128, b - a)) if diag else [(0, b - a)]
                    for pi, (pa, pl) in enumerate(pieces):
                        c.op("tensor", lambda t, kc=kc, a=a, pa=pa, pl=pl, pcol=pcol,
                             qb_buf=qb_buf, kb_buf=kb_buf: t.matmul(
                                 pcol[:, pa:pa + pl],
                                 kb_buf[0:64, kc * 128:(kc + 1) * 128],
                                 qb_buf[0:64, a + pa:a + pa + pl],
                                 start=True, stop=True),
                             sem="mm", inc=(1 if pi == len(pieces) - 1 else 0))
                    mmv = c.counts["mm"]
                    if diag:
                        c.wait("vector", "mm", mmv)
                        c.op("vector", lambda v, pcol=pcol: v.tensor_tensor(
                            pcol[:, 0:128], pcol[:, 0:128], L["MASK"][:, :],
                            OP.add), sem="vv")
                        c.wait("scalar", "vv", c.counts["vv"])
                    else:
                        c.wait("scalar", "mm", mmv)
                    if widx >= 2:
                        c.wait("scalar", "mm", pv_done_mm[widx - 2])
                    c.op("scalar", lambda s_, b=b, a=a, h=h, pcol=pcol, prbo=prbo:
                         s_.activation(BB[:, prbo:prbo + (b - a)], pcol,
                                       AF.Exp, scale=0.125,
                                       bias=L["NEGM"][:, h:h + 1]), sem="ss")
                    exp_ss[widx] = c.counts["ss"]
                    c.wait("tensor", "ss", exp_ss[widx])
                    c.op("tensor", lambda t, kc=kc, a=a, b=b, h=h, prbo=prbo: t.matmul(
                        PS_PV[0:65, a:b],
                        BB[:, VP + kc * 1040 + h * 65: VP + kc * 1040 + (h + 1) * 65],
                        BB[:, prbo:prbo + (b - a)],
                        start=(kc == 0), stop=(kc == 15)),
                        sem="mm", inc=1)
                    pv_done_mm[widx] = c.counts["mm"]
                head_mm_end.append(c.counts["mm"])
                pv_mm = c.counts["mm"]
                c.wait("scalar", "mm", pv_mm)
                c.op("scalar", lambda s_: s_.copy(BFA[64:65, YF:YF + S],
                                                  PS_PV[64:65, :]), sem="ss")
                lv = c.counts["ss"]
                c.wait("sync", "ss", lv)
                c.dma("sync", L["INVL"][h:h + 1, :], BFA[64:65, YF:YF + S], "dx")
                oc = h // 2
                odd = h % 2
                for qb in range(4):
                    if odd:
                        otdst = BB[0:64, AST + qb * 512: AST + (qb + 1) * 512]
                    else:
                        otdst = BB[0:64, OT + oc * 2048 + qb * 512:
                                   OT + oc * 2048 + (qb + 1) * 512]
                    c.op("scalar", lambda s_, otdst=otdst, qb=qb: s_.copy(
                        otdst, PS_PV[0:64, qb * 512:(qb + 1) * 512]), sem="ev")
                ot_ev = c.counts["ev"]
                c.wait("tensor", "ev", ot_ev)
                if odd:
                    c.wait("sync", "ev", ot_ev)
                    c.dma("sync", BB[64:128, OT + oc * 2048: OT + (oc + 1) * 2048],
                          BB[0:64, AST:AST + 2048], "dst")
                    c.wait("scalar", "dst", c.counts["dst"])

            c.wait("vector", "dx", c.counts["dx"])
            c.op("vector", lambda v: v.reciprocal(BFA[0:16, YF:YF + S],
                                                  L["INVL"][0:16, :]))
            c.enop("vector")
            c.op("vector", lambda v: v.tensor_tensor(
                BFA[0:16, YF + S:YF + 2 * S], L["INVL"][0:16, :],
                BFA[0:16, YF:YF + S], OP.mult))
            c.enop("vector")
            c.op("vector", lambda v: v.tensor_scalar(
                BFA[0:16, YF + S:YF + 2 * S], BFA[0:16, YF + S:YF + 2 * S],
                -1.0, 2.0, OP.mult, OP.add))
            c.enop("vector")
            c.op("vector", lambda v: v.tensor_tensor(
                L["INVL"][0:16, :], BFA[0:16, YF:YF + S],
                BFA[0:16, YF + S:YF + 2 * S], OP.mult), sem="vv")
            invl_vv = c.counts["vv"]
            c.wait("tensor", "vv", invl_vv)
            for oc in range(ND):
                for qb in range(4):
                    c.op("tensor", lambda t, oc=oc, qb=qb: t.matmul(
                        PS_X[:, 0:512], L["SELP"][:, oc * 128:(oc + 1) * 128],
                        L["INVL"][0:16, qb * 512:(qb + 1) * 512],
                        start=True, stop=True), sem="mm")
                    mmv = c.counts["mm"]
                    c.wait("scalar", "mm", mmv)
                    c.op("scalar", lambda s_: s_.copy(BFA[:, 0:512],
                                                      PS_X[:, 0:512]), sem="ss")
                    sv = c.counts["ss"]
                    c.wait("vector", "ss", sv)
                    otsl = BB[:, OT + oc * 2048 + qb * 512: OT + oc * 2048 + (qb + 1) * 512]
                    c.op("vector", lambda v, otsl=otsl: v.tensor_tensor(
                        otsl, otsl, BFA[:, 0:512], OP.mult), sem="ev")
                    evv = c.counts["ev"]
                    c.wait("tensor", "ev", evv)
                    c.wait("scalar", "ev", evv)
            otn_ev = c.counts["ev"]
            c.wait("tensor", "ev", otn_ev)
            if dump_ot:
                c.wait("sync", "ev", otn_ev)
                for ch in range(4):
                    c.op("scalar", lambda s_, ch=ch: s_.copy(
                        BFA[:, 0:4096], BB[:, OT + ch * 4096: OT + (ch + 1) * 4096]),
                        sem="ss")
                    c.wait("sync", "ss", c.counts["ss"])
                    c.dma("sync", bass.AP(L["y_out"].tensor, ch * 512 * 1024,
                                          [[1024, 512], [1, 1024]]), BFA[:, 0:4096], "dst")
                    c.wait("scalar", "dst", c.counts["dst"])
                c.dma("sync", bass.AP(L["y_out"].tensor, 2048 * 1024,
                                      [[1024, 32], [1, 1024]]), L["INVL"][:, :], "dst")
                c.wait("sync", "dst", c.counts["dst"])
                return

            fence()
            for dc in range(ND):
                c.dma("sync", BB[:, WMQ + dc * 1024: WMQ + (dc + 1) * 1024],
                      L["w_bf"]["Wm"][dc * 128:(dc + 1) * 128, :], "dw")
            dwv = c.counts["dw"]
            c.wait("tensor", "dw", dwv)
            for qb in range(4):
                pj_ev = []
                for oc in range(ND):
                    half = oc % 2
                    pcol = PS_SC[:, half * 512:(half + 1) * 512]
                    if oc >= 2:
                        c.wait("tensor", "ev", pj_ev[oc - 2])
                    for dc in range(ND):
                        c.op("tensor", lambda t, oc=oc, qb=qb, dc=dc, pcol=pcol: t.matmul(
                            pcol,
                            BB[:, WMQ + dc * 1024 + oc * 128:
                               WMQ + dc * 1024 + (oc + 1) * 128],
                            BB[:, OT + dc * 2048 + qb * 512:
                               OT + dc * 2048 + (qb + 1) * 512],
                            start=(dc == 0), stop=(dc == ND - 1)),
                            sem="mm", inc=(1 if dc == ND - 1 else 0))
                    mmv = c.counts["mm"]
                    stg = BB[:, APQ + oc * 512: APQ + (oc + 1) * 512]
                    c.wait("scalar", "mm", mmv)
                    c.op("scalar", lambda s_, pcol=pcol, stg=stg, oc=oc: s_.activation(
                        stg, pcol, AF.Identity, bias=L["BMC"][:, oc:oc + 1]), sem="ev")
                    pj_ev.append(c.counts["ev"])
                apq_ev = c.counts["ev"]
                c.wait("tensor", "ev", apq_ev)
                for jj in range(4):
                    j = soff // 128 + qb * 4 + jj
                    xb = BFA[:, XF + (jj % 2) * 1024: XF + (jj % 2) * 1024 + 1024]
                    if qb > 0 or jj >= 2:
                        c.wait("sync", "dst", c.counts["dst"])
                    c.dma("sync", xb, L["xs"][j * 128:(j + 1) * 128, :], "dx")
                    dxv = c.counts["dx"]
                    for oc in range(ND):
                        c.op("tensor", lambda t, oc=oc, jj=jj: t.transpose(
                            PS_T[:, oc * 128:(oc + 1) * 128],
                            BB[:, APQ + oc * 512 + jj * 128:
                               APQ + oc * 512 + jj * 128 + 128],
                            L["ID"][:, :]), sem="mm", inc=(1 if oc == ND - 1 else 0))
                    mmv = c.counts["mm"]
                    c.wait("vector", "mm", mmv)
                    c.wait("vector", "dx", dxv)
                    c.op("vector", lambda v, xb=xb: v.tensor_tensor(
                        xb, xb, PS_T[:, :], OP.add), sem="vv")
                    vvv = c.counts["vv"]
                    c.wait("tensor", "vv", vvv)
                    c.wait("sync", "vv", vvv)
                    c.dma("sync", dst[j * 128:(j + 1) * 128, :], xb, "dst")

    if upto == "dumpB":
        ffn(0, L["x_in"], L["y_out"], stop_at="B")
        c.wait("sync", "dst", c.counts["dst"])
        for ic in range(8):
            buf = BB[:, 0:4096]
            f32buf = BFA[:, 0:4096]
            c.dma("sync", buf, L["sp"]["up"][ic * 128:(ic + 1) * 128, :], "dsp")
            c.wait("scalar", "dsp", c.counts["dsp"])
            c.op("scalar", lambda s_: s_.copy(f32buf, buf), sem="ss")
            c.wait("sync", "ss", c.counts["ss"])
            c.dma("sync", bass.AP(L["y_out"].tensor, ic * 512 * 1024,
                                  [[1024, 512], [1, 1024]]), f32buf, "dst")
            c.wait("sync", "dst", c.counts["dst"])
        c.wait("sync", "dst", c.counts["dst"])
        return
    if upto in ("p0", "ffnA", "ffnB", "ffnC", "ffnD", "ffnE0", "ffnE1", "ffnEZ", "ffnEY"):
        if upto == "p0":
            for j in range(NT):
                c.dma("sync", L["y_out"][j * 128:(j + 1) * 128, :],
                      L["x_in"][j * 128:(j + 1) * 128, :], "dst")
        else:
            ffn(0, L["x_in"], L["y_out"], stop_at=upto[3:])
    elif upto == "ffn1":
        ffn(0, L["x_in"], L["y_out"])
    elif upto == "attn":
        ffn(0, L["x_in"], L["xs"])
        attn(L["y_out"])
    elif upto == "dumpOT":
        ffn(0, L["x_in"], L["xs"])
        attn(L["y_out"], dump_ot=True)
    else:
        ffn(0, L["x_in"], L["xs"])
        attn(L["xs"])
        ffn(1, L["xs"], L["y_out"])
    c.wait("sync", "dst", c.counts["dst"])



def _consts():
    ident_bf = np.eye(128, dtype=ml_dtypes.bfloat16)
    m = np.zeros((128, 128), dtype=np.float32)
    m[np.arange(128)[:, None] > np.arange(128)[None, :]] = -1e12
    selh = np.zeros((16, H * HS), dtype=np.float32)
    for h in range(H):
        selh[h, h * HS:(h + 1) * HS] = 1.0
    selp = np.zeros((16, 8 * 128), dtype=np.float32)
    for oc in range(8):
        selp[2 * oc, oc * 128:oc * 128 + 64] = 1.0
        selp[2 * oc + 1, oc * 128 + 64:oc * 128 + 128] = 1.0
    return ident_bf, m, selh, selp


def _in_maps(x, Wq, bq, Wk, bk, Wv, bv, Wm, bm, w_gate, w_up, w_down):
    ident_bf, maskt, selh, selp = _consts()
    f32 = lambda a: np.ascontiguousarray(np.asarray(a, dtype=np.float32))
    base = dict(Wq=f32(Wq), bq=f32(bq), Wk=f32(Wk), bk=f32(bk), Wv=f32(Wv),
                bv=f32(bv), Wm=f32(Wm), bm=f32(bm), w_gate=f32(w_gate),
                w_up=f32(w_up), w_down=f32(w_down),
                ident_bf=ident_bf, maskt=maskt, selh=selh, selp=selp)
    maps = []
    for core in range(8):
        m = dict(base)
        m["x"] = np.ascontiguousarray(
            np.asarray(x, dtype=np.float32)[2 * core:2 * core + 2].reshape(T, D))
        maps.append(m)
    return maps


def kernel(L0_attn_x, L0_attn_Wq, L0_attn_bq, L0_attn_Wk, L0_attn_bk,
           L0_attn_Wv, L0_attn_bv, L0_attn_Wm, L0_attn_bm,
           L0_ffn_w_gate, L0_ffn_w_up, L0_ffn_w_down):
    nc = build_nc()
    maps = _in_maps(L0_attn_x, L0_attn_Wq, L0_attn_bq, L0_attn_Wk, L0_attn_bk,
                    L0_attn_Wv, L0_attn_bv, L0_attn_Wm, L0_attn_bm,
                    L0_ffn_w_gate, L0_ffn_w_up, L0_ffn_w_down)
    res = run_bass_kernel_spmd(nc, maps, core_ids=list(range(8)))
    out = np.empty((16, 2048, 1024), dtype=np.float32)
    for core in range(8):
        out[2 * core:2 * core + 2] = np.asarray(
            res.results[core]["y"]).reshape(2, 2048, 1024)
    return out


# version 10
# speedup vs baseline: 1.0436x; 1.0436x over previous
import numpy as np
import ml_dtypes
import concourse.bass as bass
import concourse.mybir as mybir
from concourse.alu_op_type import AluOpType
from concourse.bass_utils import run_bass_kernel_spmd
from contextlib import ExitStack

F32 = mybir.dt.float32
BF16 = mybir.dt.bfloat16
AF = mybir.ActivationFunctionType
AX = mybir.AxisListType
OP = AluOpType

K23 = float(1.5 * 2.0 ** 23)
K52 = float(1.5 * 2.0 ** 52)

T, D, I, S = 4096, 1024, 3072, 2048
NSEQ, ND, NI, NT, H, HS = 2, 8, 24, 32, 16, 64
EPS = 1e-5

ENGS = ("sync", "tensor", "vector", "scalar", "gpsimd")


class Ctx:

    def __init__(self, nc, sems):
        self.nc = nc
        self.sems = sems
        self.eng = {}
        self.cur = None
        self.counts = None

    def start(self, name, handle):
        self.cur = name
        self.handle = handle
        self.counts = {k: 0 for k in self.sems}

    def dma(self, q, out, in_, sem, transpose=False, slow=False):
        self.counts[sem] += 16
        if self.cur == q:
            if transpose:
                self.handle.dma_start_transpose(out, in_).then_inc(self.sems[sem], 16)
            else:
                self.handle.dma_start(out=out, in_=in_,
                                      allow_slow_non_contiguous=slow).then_inc(
                    self.sems[sem], 16)
        return self.counts[sem]

    def op(self, ename, fn, sem=None, inc=1):
        if sem is not None:
            self.counts[sem] += inc
        if self.cur == ename:
            ins = fn(self.handle)
            if sem is not None and inc > 0:
                ins.then_inc(self.sems[sem], inc)
        return self.counts[sem] if sem is not None else None

    def enop(self, ename, n=2):
        if self.cur == ename:
            for _ in range(n):
                self.handle.engine_nop()

    def wait(self, ename, sem, val):
        if self.cur == ename:
            self.handle.wait_ge(self.sems[sem], val)

    def cc(self, ename, op, ins, outs, sem):
        self.counts[sem] += 1
        if self.cur == ename:
            self.handle.collective_compute(
                "AllReduce", op, replica_groups=[list(range(8))], ins=ins, outs=outs
            ).then_inc(self.sems[sem], 1)
        return self.counts[sem]


def bank_splits(a, b, align=512):
    out = []
    while a < b:
        e = min(b, (a // align + 1) * align)
        out.append((a, e - a))
        a = e
    return out


def build_nc(upto=None):
    nc = bass.Bass("TRN2", target_bir_lowering=False)

    x_in = nc.dram_tensor("x", [T, D], F32, kind="ExternalInput").ap()
    w_in = {}
    for nm, sh in (("Wq", [D, D]), ("Wk", [D, D]), ("Wv", [D, D]), ("Wm", [D, D]),
                   ("w_gate", [I, D]), ("w_up", [I, D]), ("w_down", [D, I])):
        w_in[nm] = nc.dram_tensor(nm, sh, F32, kind="ExternalInput").ap()
    b_in = {nm: nc.dram_tensor(nm, [D], F32, kind="ExternalInput").ap()
            for nm in ("bq", "bk", "bv", "bm")}
    ident_bf_in = nc.dram_tensor("ident_bf", [128, 128], BF16, kind="ExternalInput").ap()
    maskt_in = nc.dram_tensor("maskt", [128, 128], F32, kind="ExternalInput").ap()
    sel_in = nc.dram_tensor("selh", [16, H * HS], F32, kind="ExternalInput").ap()
    selp_in = nc.dram_tensor("selp", [16, 8 * 128], F32, kind="ExternalInput").ap()
    y_out = nc.dram_tensor("y", [T, D], F32, kind="ExternalOutput").ap()

    xs = nc.dram_tensor("xs", [T, D], F32).ap()
    w_bf = {}
    for nm, sh in (("Wq", [D, D]), ("Wk", [D, D]), ("Wv", [D, D]), ("Wm", [D, D]),
                   ("w_gate", [I, D]), ("w_up", [I, D]), ("w_down", [D, I])):
        w_bf[nm] = nc.dram_tensor(nm + "_bf", sh, BF16).ap()
    sp = {nm: nc.dram_tensor("sp_" + nm, [I, T], BF16).ap()
          for nm in ("up", "gate", "p2", "p3")}
    qsp = nc.dram_tensor("qsp", [D, S], BF16).ap()
    ksp = nc.dram_tensor("ksp", [D, S], BF16).ap()
    cc_i = nc.dram_tensor("cc_i", [1, 16], F32).ap()
    cc_o = nc.dram_tensor("cc_o", [1, 16], F32).ap()

    st = ExitStack()
    with st:
        sb = lambda n_, sh, dt: st.enter_context(nc.sbuf_tensor(n_, sh, dt))
        L = dict(x_in=x_in, w_in=w_in, b_in=b_in, y_out=y_out, xs=xs, w_bf=w_bf,
                 sp=sp, qsp=qsp, ksp=ksp, cc_i=cc_i, cc_o=cc_o,
                 ident_bf_in=ident_bf_in, maskt_in=maskt_in, sel_in=sel_in,
                 selp_in=selp_in, upto=upto)
        L["BB"] = sb("BB", [128, 65536], BF16)
        L["BF"] = sb("BFA", [128, 8192], F32)
        L["ID"] = sb("ID", [128, 128], BF16)
        L["MASK"] = sb("MASK", [128, 128], F32)
        L["SEL"] = sb("SEL", [16, H * HS], F32)
        L["SELP"] = sb("SELP", [16, 8 * 128], F32)
        L["ONES"] = sb("ONESF", [128, 128], F32)
        L["KCOL"] = sb("KCOL", [128, 1], F32)
        L["SC"] = sb("SCC", [128, 32], F32)
        L["BQC"] = sb("BQC", [128, 8], F32)
        L["BKC"] = sb("BKC", [128, 8], F32)
        L["BMC"] = sb("BMC", [128, 8], F32)
        L["BVR1"] = sb("BVR1", [1, D], F32)
        L["BVROW"] = sb("BVROW", [128, D], F32)
        L["STAT"] = sb("STAT", [128, 256], F32)
        L["UGC"] = sb("UGC", [128, 448], F32)
        L["CCIN"] = sb("CCIN", [1, 16], F32)
        L["CCOUT"] = sb("CCOUT", [1, 16], F32)
        L["SM"] = sb("SM", [128, 64], F32)
        L["INVL"] = sb("INVL", [16, S], F32)
        L["NEGM"] = sb("NEGM", [128, 16], F32)
        L["BCOL"] = sb("BCOL", [128, 1], BF16)
        L["PS_SC"] = st.enter_context(nc.psum_tensor("PS_SC", [128, 1024], F32))
        L["PS_PV"] = st.enter_context(nc.psum_tensor("PS_PV", [128, 2048], F32))
        L["PS_X"] = st.enter_context(nc.psum_tensor("PS_X", [128, 512], F32))
        L["PS_T"] = st.enter_context(nc.psum_tensor("PS_T", [128, 1024], BF16))

        sems = {}
        for s_ in ("dx", "dw", "dsp", "dst", "mm", "ev", "vv", "ss", "gg", "cc", "dq",
                   "fsy", "fte", "fve", "fsc", "fgp"):
            sems[s_] = st.enter_context(nc.semaphore("sem_" + s_))
        block = st.enter_context(nc.Block())
        ctx = Ctx(nc, sems)

        @block.sync
        def _(eng):
            ctx.start("sync", eng)
            emit_program(ctx, L)

        @block.tensor
        def _(eng):
            ctx.start("tensor", eng)
            emit_program(ctx, L)

        @block.vector
        def _(eng):
            ctx.start("vector", eng)
            emit_program(ctx, L)

        @block.scalar
        def _(eng):
            ctx.start("scalar", eng)
            emit_program(ctx, L)

        @block.gpsimd
        def _(eng):
            ctx.start("gpsimd", eng)
            emit_program(ctx, L)

    return nc


XQT = 0
WUG = 32768
STG = 57344
CD_IN = 0
CD_OUT = 8192
CD_UP = 16384
ORQ, WDT, PIN, FFO = 0, 24576, 49152, 53248
QKH = 0
VP = 8192
PRB = 24832
OT = 26624
LN2T = 43008
WMQ = 43008
APQ = 51200
WAT = 59392
AST = 63488

XF = 0
XF2 = 2048
YF = 4096


def emit_program(c, L):
    BB, BFA, SC, STAT, UGC, SM = L["BB"], L["BF"], L["SC"], L["STAT"], L["UGC"], L["SM"]
    PS_SC, PS_PV, PS_X, PS_T = L["PS_SC"], L["PS_PV"], L["PS_X"], L["PS_T"]
    KCOL, ONES, CCIN, CCOUT = L["KCOL"], L["ONES"], L["CCIN"], L["CCOUT"]
    upto = L["upto"]

    S0C, S0IC, SGC, SGIC, SG65C, SUIC, S2IC, S2UC, S3IC, S3HC = range(10)

    def sc(i):
        return SC[:, i:i + 1]

    def fence():
        mk = {"sync": "fsy", "tensor": "fte", "vector": "fve",
              "scalar": "fsc", "gpsimd": "fgp"}
        for e in ENGS:
            c.op(e, lambda x: x.nop(), sem=mk[e])
        vals = {e: c.counts[mk[e]] for e in ENGS}
        for e in ENGS:
            for o in ENGS:
                if o != e:
                    c.wait(e, mk[o], vals[o])

    c.dma("sync", L["ID"][:, :], L["ident_bf_in"], "dw")
    c.dma("sync", L["MASK"][:, :], L["maskt_in"], "dw")
    c.dma("sync", L["SEL"][:, :], L["sel_in"], "dw")
    c.dma("sync", L["SELP"][:, :], L["selp_in"], "dw")
    for nm, tgt in (("bq", L["BQC"]), ("bk", L["BKC"]), ("bm", L["BMC"])):
        c.dma("sync", tgt[:, :],
              bass.AP(L["b_in"][nm].tensor, 0, [[1, 128], [128, 8]]), "dw", slow=True)
    c.dma("sync", L["BVR1"][:, :],
          bass.AP(L["b_in"]["bv"].tensor, 0, [[1024, 1], [1, 1024]]), "dw")
    consts_dw = c.counts["dw"]

    c.op("vector", lambda v: v.memset(ONES[:, :], 1.0))
    c.enop("vector")
    c.op("vector", lambda v: v.tensor_scalar(SM[:, 0:1], ONES[:, 0:1], K52, None, OP.add))
    c.enop("vector")
    c.op("vector", lambda v: v.tensor_scalar(SM[:, 1:2], SM[:, 0:1], K52, None, OP.subtract))
    c.enop("vector")
    c.op("vector", lambda v: v.tensor_scalar(KCOL[:, :], SM[:, 1:2], K52, K23, OP.mult, OP.add))
    c.enop("vector")
    c.op("vector", lambda v: v.memset(CCIN[:, :], 0.0))
    c.op("vector", lambda v: v.memset(L["INVL"][:, :], 0.0), sem="vv")
    kblend_vv = c.counts["vv"]

    c.wait("tensor", "dw", consts_dw)
    c.wait("tensor", "vv", kblend_vv)
    for half in range(2):
        c.op("tensor", lambda t, h=half: t.matmul(
            PS_X[:, 0:512], ONES[0:1, :], L["BVR1"][0:1, h * 512:(h + 1) * 512],
            start=True, stop=True), sem="mm")
        mmv = c.counts["mm"]
        c.wait("scalar", "mm", mmv)
        c.op("scalar", lambda s_, h=half: s_.copy(
            L["BVROW"][:, h * 512:(h + 1) * 512], PS_X[:, 0:512]), sem="ss")
        c.wait("tensor", "ss", c.counts["ss"])
    bvrow_ss = c.counts["ss"]

    cast_guard = []
    ci = 0
    for nm in ("Wq", "Wk", "Wv", "Wm"):
        for ch in range(ND):
            buf = BB[:, WUG + (ci % 2) * 3072: WUG + (ci % 2) * 3072 + D]
            if ci >= 2:
                c.wait("gpsimd", "dst", cast_guard[ci - 2])
            c.dma("gpsimd", buf, L["w_in"][nm][ch * 128:(ch + 1) * 128, :], "dq")
            c.wait("sync", "dq", c.counts["dq"])
            c.dma("sync", L["w_bf"][nm][ch * 128:(ch + 1) * 128, :], buf, "dst")
            cast_guard.append(c.counts["dst"])
            ci += 1

    for nm, rows, cols in (("w_up", I, D), ("w_gate", I, D), ("w_down", D, I)):
        nb = cols // 32
        for ch in range(rows // 128):
            WF = BFA[:, 0:cols]
            YW = BFA[:, 4096:4096 + cols]
            c.wait("sync", "vv", c.counts["vv"])
            c.dma("sync", WF, L["w_in"][nm][ch * 128:(ch + 1) * 128, :], "dx")
            dxv = c.counts["dx"]
            c.wait("vector", "dx", dxv)
            c.op("vector", lambda v, WF=WF, nb=nb: v.tensor_reduce(
                UGC[:, 0:nb], WF.rearrange("p (a b) -> p a b", a=nb),
                AX.X, OP.max, apply_absolute_value=True), sem="vv")
            c.enop("vector")
            c.op("vector", lambda v, nb=nb: v.tensor_scalar(
                UGC[:, 96:96 + nb], UGC[:, 0:nb], 1.0 / 7.0, 1e-12, OP.mult, OP.max))
            c.enop("vector")
            c.op("vector", lambda v, nb=nb: v.reciprocal(UGC[:, 192:192 + nb],
                                                         UGC[:, 96:96 + nb]))
            c.enop("vector")
            c.op("vector", lambda v, nb=nb: v.tensor_tensor(
                UGC[:, 288:288 + nb], UGC[:, 96:96 + nb], UGC[:, 192:192 + nb],
                OP.mult))
            c.enop("vector")
            c.op("vector", lambda v, nb=nb: v.tensor_scalar(
                UGC[:, 288:288 + nb], UGC[:, 288:288 + nb], -1.0, 2.0, OP.mult, OP.add))
            c.enop("vector")
            c.op("vector", lambda v, nb=nb: v.tensor_tensor(
                UGC[:, 192:192 + nb], UGC[:, 192:192 + nb], UGC[:, 288:288 + nb],
                OP.mult))
            c.enop("vector")
            def bc0(col0, nb):
                a = UGC[:, col0:col0 + nb]
                return bass.AP(a.tensor, a.offset, [a.ap[0], [1, nb], [0, 32]])
            c.op("vector", lambda v, WF=WF, YW=YW, nb=nb: v.tensor_tensor(
                YW.rearrange("p (a b) -> p a b", a=nb),
                WF.rearrange("p (a b) -> p a b", a=nb), bc0(192, nb), OP.mult))
            c.op("vector", lambda v, WF=WF, YW=YW: v.tensor_scalar(
                WF, YW, KCOL[:, 0:1], None, OP.add))
            c.op("vector", lambda v, WF=WF, YW=YW: v.tensor_scalar(
                YW, WF, KCOL[:, 0:1], None, OP.subtract))
            stgw = BB[:, STG:STG + cols]
            c.op("vector", lambda v, YW=YW, stgw=stgw, nb=nb: v.tensor_tensor(
                stgw.rearrange("p (a b) -> p a b", a=nb),
                YW.rearrange("p (a b) -> p a b", a=nb), bc0(96, nb), OP.mult),
                sem="vv")
            vvw = c.counts["vv"]
            c.wait("sync", "vv", vvw)
            c.dma("sync", L["w_bf"][nm][ch * 128:(ch + 1) * 128, :], stgw, "dst")
    wcast_dst = c.counts["dst"]
    fence()

    def barrier():
        c.wait("sync", "vv", c.counts["vv"])
        c.wait("sync", "dst", c.counts["dst"])
        c.dma("sync", L["cc_i"], CCIN[:, :], "dst")
        c.wait("gpsimd", "dst", c.counts["dst"])
        c.cc("gpsimd", OP.max, [L["cc_i"]], [L["cc_o"]], "cc")
        c.wait("sync", "cc", c.counts["cc"])
        c.dma("sync", CCOUT[:, :], L["cc_o"], "dx")
        return c.counts["dx"]

    def bcast_scalar(src_ap, dst_slot):
        c.op("tensor", lambda t: t.matmul(PS_X[:, 0:1], ONES[0:1, :], src_ap,
                                          start=True, stop=True), sem="mm")
        c.wait("scalar", "mm", c.counts["mm"])
        c.op("scalar", lambda s_: s_.copy(sc(dst_slot), PS_X[:, 0:1]), sem="ss")
        c.wait("tensor", "ss", c.counts["ss"])

    def partition_absmax_to_ccin(colvec_ap, ccslot):
        c.op("vector", lambda v: v.tensor_copy(L["BCOL"][:, :], colvec_ap), sem="vv")
        c.wait("tensor", "vv", c.counts["vv"])
        c.op("tensor", lambda t: t.transpose(PS_T[0:1, 0:128], L["BCOL"][:, :],
                                             L["ID"][:, :]), sem="mm")
        c.wait("vector", "mm", c.counts["mm"])
        c.op("vector", lambda v: v.tensor_reduce(
            CCIN[0:1, ccslot:ccslot + 1], PS_T[0:1, 0:128], AX.X, OP.max), sem="vv")
        c.wait("tensor", "vv", c.counts["vv"])

    def ln_stats(src, soff, ntile, with_minmax):
        guards = []
        for j in range(ntile):
            jt = soff // 128 + j
            xb = BFA[:, XF + (j % 2) * 1024: XF + (j % 2) * 1024 + 1024]
            if j >= 2:
                gv, gs = guards[j - 2]
                c.wait("sync", "vv", gv)
                c.wait("sync", "ss", gs)
            c.dma("sync", xb, src[jt * 128:(jt + 1) * 128, :], "dx")
            dxv = c.counts["dx"]
            c.wait("vector", "dx", dxv)
            c.op("vector", lambda v, xb=xb, j=j: v.tensor_scalar(
                BFA[:, YF:YF + 1024], xb, 1.0, None, OP.mult, OP.add,
                accum_out=STAT[:, j:j + 1]), sem="vv")
            if with_minmax:
                c.op("vector", lambda v, xb=xb, j=j: v.tensor_reduce(
                    STAT[:, 64 + j:65 + j], xb, AX.X, OP.max), sem="vv")
                c.op("vector", lambda v, xb=xb, j=j: v.tensor_reduce(
                    STAT[:, 96 + j:97 + j], xb, AX.X, OP.min), sem="vv")
            c.wait("scalar", "dx", dxv)
            c.op("scalar", lambda s_, xb=xb, j=j: s_.activation(
                BFA[:, YF + 2048:YF + 3072], xb, AF.Square,
                accum_out=STAT[:, 32 + j:33 + j]), sem="ss")
            guards.append((c.counts["vv"], c.counts["ss"]))
        c.wait("vector", "vv", c.counts["vv"])
        c.wait("vector", "ss", c.counts["ss"])
        c.op("vector", lambda v: v.tensor_scalar(STAT[:, 128:128 + ntile],
                                                 STAT[:, 0:ntile], 1.0 / D, None, OP.mult))
        c.enop("vector")
        c.op("vector", lambda v: v.tensor_scalar(STAT[:, 160:160 + ntile],
                                                 STAT[:, 32:32 + ntile], 1.0 / D, None, OP.mult))
        c.enop("vector")
        c.op("vector", lambda v: v.scalar_tensor_tensor(
            SM[:, 32:32 + ntile], STAT[:, 128:128 + ntile], -1.0,
            STAT[:, 128:128 + ntile], OP.mult, OP.mult))
        c.enop("vector")
        c.op("vector", lambda v: v.tensor_tensor(STAT[:, 160:160 + ntile],
                                                 STAT[:, 160:160 + ntile],
                                                 SM[:, 32:32 + ntile], OP.add))
        c.enop("vector")
        c.op("vector", lambda v: v.tensor_scalar(STAT[:, 160:160 + ntile],
                                                 STAT[:, 160:160 + ntile], EPS, None, OP.add))
        c.enop("vector")
        c.op("vector", lambda v: v.reciprocal(STAT[:, 160:160 + ntile],
                                              STAT[:, 160:160 + ntile]), sem="vv")
        c.wait("scalar", "vv", c.counts["vv"])
        c.op("scalar", lambda s_: s_.activation(STAT[:, 160:160 + ntile],
                                                STAT[:, 160:160 + ntile], AF.Sqrt), sem="ss")
        return c.counts["ss"]

    def ffn(call, src, dst, stop_at=None):
        def _stop_copy():
            for j in range(NT):
                c.dma("sync", dst[j * 128:(j + 1) * 128, :],
                      src[j * 128:(j + 1) * 128, :], "dst")
        base = 0 if call == 0 else 5
        fence()
        c.wait("sync", "dst", c.counts["dst"])
        rstd_ss = ln_stats(src, 0, 32, True)
        c.wait("vector", "ss", rstd_ss)
        c.op("vector", lambda v: v.tensor_tensor(SM[:, 0:32], STAT[:, 64:96],
                                                 STAT[:, 128:160], OP.subtract))
        c.enop("vector")
        c.op("vector", lambda v: v.tensor_tensor(SM[:, 32:64], STAT[:, 128:160],
                                                 STAT[:, 96:128], OP.subtract))
        c.enop("vector")
        c.op("vector", lambda v: v.tensor_tensor(SM[:, 0:32], SM[:, 0:32],
                                                 SM[:, 32:64], OP.max))
        c.enop("vector")
        c.op("vector", lambda v: v.tensor_tensor(SM[:, 0:32], SM[:, 0:32],
                                                 STAT[:, 160:192], OP.mult))
        c.enop("vector")
        c.op("vector", lambda v: v.tensor_reduce(SM[:, 40:41], SM[:, 0:32],
                                                 AX.X, OP.max), sem="vv")
        c.enop("vector")
        partition_absmax_to_ccin(SM[:, 40:41], base + 0)
        bval = barrier()

        c.wait("vector", "dx", bval)
        c.op("vector", lambda v, b=base: v.tensor_scalar(
            SM[0:1, 0:1], CCOUT[0:1, b:b + 1], 1.0 / 32767.0, 1e-12,
            OP.mult, OP.max))
        c.enop("vector")
        c.op("vector", lambda v: v.reciprocal(SM[0:1, 1:2], SM[0:1, 0:1]), sem="vv")
        c.enop("vector")
        c.wait("tensor", "vv", c.counts["vv"])
        bcast_scalar(SM[0:1, 0:1], S0C)
        bcast_scalar(SM[0:1, 1:2], S0IC)
        if stop_at == "A":
            _stop_copy(); return
        c.wait("vector", "ss", c.counts["ss"])
        c.op("vector", lambda v: v.tensor_scalar(STAT[:, 192:224], STAT[:, 160:192],
                                                 sc(S0IC), None, OP.mult))
        c.enop("vector")
        c.op("vector", lambda v: v.scalar_tensor_tensor(
            STAT[:, 224:256], STAT[:, 128:160], -1.0, STAT[:, 192:224],
            OP.mult, OP.mult), sem="vv")
        scl_vv = c.counts["vv"]

        c.wait("scalar", "vv", scl_vv)
        xq_guard = []
        for j in range(NT):
            xb = BFA[:, XF + (j % 2) * 1024: XF + (j % 2) * 1024 + 1024]
            if j >= 2:
                c.wait("sync", "ss", xq_guard[j - 2])
            c.dma("sync", xb, src[j * 128:(j + 1) * 128, :], "dx")
            dxv = c.counts["dx"]
            c.wait("scalar", "dx", dxv)
            c.op("scalar", lambda s_, xb=xb, j=j: s_.activation(
                BFA[:, YF:YF + 1024], xb, AF.Identity,
                bias=STAT[:, 224 + j:225 + j], scale=STAT[:, 192 + j:193 + j]), sem="ss")
            sv = c.counts["ss"]
            c.wait("vector", "ss", sv)
            c.op("vector", lambda v: v.tensor_scalar(
                BFA[:, YF + 1024:YF + 2048], BFA[:, YF:YF + 1024], KCOL[:, 0:1],
                None, OP.add))
            c.op("vector", lambda v: v.tensor_scalar(
                BB[:, STG:STG + 1024], BFA[:, YF + 1024:YF + 2048], KCOL[:, 0:1],
                None, OP.subtract), sem="vv")
            vv0 = c.counts["vv"]
            c.wait("tensor", "vv", vv0)
            for dc in range(ND):
                c.op("tensor", lambda t, dc=dc: t.transpose(
                    PS_T[:, dc * 128:(dc + 1) * 128],
                    BB[:, STG + dc * 128: STG + (dc + 1) * 128], L["ID"][:, :]),
                    sem="mm", inc=(1 if dc == ND - 1 else 0))
            mmv = c.counts["mm"]
            c.wait("scalar", "mm", mmv)
            out3 = BB[:, XQT:XQT + 32768].rearrange("p (a b) -> p a b", a=8)[
                :, :, j * 128:(j + 1) * 128]
            c.op("scalar", lambda s_, out3=out3: s_.copy(
                out3, PS_T[:, :].rearrange("p (a b) -> p a b", a=8)), sem="ss")
            xq_guard.append(c.counts["ss"])
            c.wait("tensor", "ss", c.counts["ss"])
        xq_ss = c.counts["ss"]

        for wi, wname in enumerate(("w_up", "w_gate")):
            for dc in range(ND):
                c.dma("sync", BB[:, WUG + dc * I: WUG + (dc + 1) * I],
                      L["w_bf"][wname][:, dc * 128:(dc + 1) * 128], "dw", transpose=True)
            dwv = c.counts["dw"]
            c.wait("sync", "dw", dwv)
            c.wait("tensor", "dw", dwv)
            c.wait("tensor", "ss", xq_ss)
            grp_ev, grp_dst, grp_vv = [], [], []
            for g in range(NI * 8):
                ic, tb = divmod(g, 8)
                half = g % 2
                pcol = PS_SC[:, half * 512:(half + 1) * 512]
                stg = BB[:, STG + half * 512: STG + half * 512 + 512]
                if g >= 2:
                    c.wait("tensor", "ev", grp_ev[g - 2])
                for dc in range(ND):
                    c.op("tensor", lambda t, ic=ic, tb=tb, dc=dc, pcol=pcol: t.matmul(
                        pcol,
                        BB[:, WUG + dc * I + ic * 128: WUG + dc * I + (ic + 1) * 128],
                        BB[:, XQT + dc * 4096 + tb * 512: XQT + dc * 4096 + (tb + 1) * 512],
                        start=(dc == 0), stop=(dc == ND - 1)),
                        sem="mm", inc=(1 if dc == ND - 1 else 0))
                mmv = c.counts["mm"]
                c.wait("scalar", "mm", mmv)
                if g >= 2:
                    c.wait("scalar", "dst", grp_dst[g - 2])
                    c.wait("scalar", "vv", grp_vv[g - 2])
                c.op("scalar", lambda s_, pcol=pcol, stg=stg: s_.activation(
                    stg, pcol, AF.Identity, scale=sc(S0C)), sem="ev")
                evv = c.counts["ev"]
                grp_ev.append(evv)
                c.wait("vector", "ev", evv)
                cidx = wi * 192 + g
                c.op("vector", lambda v, stg=stg, cidx=cidx: v.tensor_reduce(
                    UGC[:, cidx:cidx + 1], stg, AX.X, OP.max,
                    apply_absolute_value=True), sem="vv")
                grp_vv.append(c.counts["vv"])
                c.wait("sync", "ev", evv)
                c.dma("sync", L["sp"][{"w_up": "up", "w_gate": "gate"}[wname]][
                    ic * 128:(ic + 1) * 128, tb * 512:(tb + 1) * 512], stg, "dst")
                grp_dst.append(c.counts["dst"])
        c.wait("vector", "vv", c.counts["vv"])
        c.op("vector", lambda v: v.tensor_reduce(SM[:, 41:42], UGC[:, 0:192],
                                                 AX.X, OP.max), sem="vv")
        c.enop("vector")
        c.op("vector", lambda v: v.tensor_reduce(SM[:, 42:43], UGC[:, 192:384],
                                                 AX.X, OP.max), sem="vv")
        c.enop("vector")
        partition_absmax_to_ccin(SM[:, 41:42], base + 1)
        partition_absmax_to_ccin(SM[:, 42:43], base + 2)
        bval = barrier()

        c.wait("vector", "dx", bval)
        c.op("vector", lambda v, b=base: v.tensor_scalar(
            SM[0:1, 2:3], CCOUT[0:1, b + 1:b + 2], 1.0 / 32767.0, 1e-12, OP.mult, OP.max))
        c.enop("vector")
        c.op("vector", lambda v, b=base: v.tensor_scalar(
            SM[0:1, 3:4], CCOUT[0:1, b + 2:b + 3], 1.0 / 32767.0, 1e-12, OP.mult, OP.max))
        c.enop("vector")
        c.op("vector", lambda v: v.reciprocal(SM[0:1, 4:5], SM[0:1, 2:3]))
        c.enop("vector")
        c.op("vector", lambda v: v.reciprocal(SM[0:1, 5:6], SM[0:1, 3:4]))
        c.enop("vector")
        c.op("vector", lambda v: v.tensor_scalar(SM[0:1, 6:7], SM[0:1, 3:4],
                                                 1.0 / 65536.0, None, OP.mult), sem="vv")
        c.wait("tensor", "vv", c.counts["vv"])
        bcast_scalar(SM[0:1, 3:4], SGC)
        bcast_scalar(SM[0:1, 5:6], SGIC)
        bcast_scalar(SM[0:1, 6:7], SG65C)
        bcast_scalar(SM[0:1, 4:5], SUIC)
        if stop_at == "B":
            _stop_copy(); return

        cd_guard = []
        for ti in range(NI * 2):
            ic, th = divmod(ti, 2)
            gin = BB[:, CD_IN + (ti % 2) * 2048: CD_IN + (ti % 2) * 2048 + 2048]
            pout = BB[:, CD_OUT + (ti % 2) * 2048: CD_OUT + (ti % 2) * 2048 + 2048]
            if ti >= 2:
                gdst, ggv = cd_guard[ti - 2]
                c.wait("sync", "dst", gdst)
            c.dma("sync", gin, L["sp"]["gate"][ic * 128:(ic + 1) * 128,
                                               th * 2048:(th + 1) * 2048], "dsp")
            dv = c.counts["dsp"]
            Y = BFA[:, 0:2048]
            RG = BFA[:, 2048:4096]
            SGm = BFA[:, 4096:6144]
            Y2 = BFA[:, 6144:8192]
            c.wait("scalar", "dsp", dv)
            c.op("scalar", lambda s_, gin=gin, Y=Y: s_.activation(
                Y, gin, AF.Identity, bias=KCOL[:, 0:1], scale=sc(SGIC)), sem="ss")
            sv = c.counts["ss"]
            c.wait("vector", "ss", sv)
            c.op("vector", lambda v, Y=Y, RG=RG: v.tensor_scalar(
                RG, Y, KCOL[:, 0:1], None, OP.subtract), sem="vv")
            vv1 = c.counts["vv"]
            c.wait("scalar", "vv", vv1)
            c.op("scalar", lambda s_, RG=RG, SGm=SGm: s_.activation(
                SGm, RG, AF.Sigmoid, scale=sc(SGC)), sem="ss")
            sv2 = c.counts["ss"]
            c.wait("vector", "ss", sv2)
            c.op("vector", lambda v, SGm=SGm, Y2=Y2: v.tensor_scalar(
                Y2, SGm, 65536.0, KCOL[:, 0:1], OP.mult, OP.add))
            c.op("vector", lambda v, Y2=Y2: v.tensor_scalar(
                Y2, Y2, KCOL[:, 0:1], 65535.0, OP.subtract, OP.min), sem="vv")
            vv2 = c.counts["vv"]
            if ti >= 2:
                c.wait("vector", "dst", cd_guard[ti - 2][0])
            c.op("vector", lambda v, RG=RG, Y2=Y2, pout=pout: v.scalar_tensor_tensor(
                pout, RG, sc(SG65C), Y2, OP.mult, OP.mult), sem="gg")
            gv = c.counts["gg"]
            c.wait("vector", "gg", gv)
            c.op("vector", lambda v, pout=pout, ti=ti: v.tensor_reduce(
                UGC[:, 384 + ti:385 + ti], pout, AX.X, OP.max,
                apply_absolute_value=True), sem="vv")
            c.wait("sync", "gg", gv)
            c.dma("sync", L["sp"]["p2"][ic * 128:(ic + 1) * 128,
                                        th * 2048:(th + 1) * 2048], pout, "dst")
            cd_guard.append((c.counts["dst"], c.counts["vv"]))
        c.wait("vector", "vv", c.counts["vv"])
        c.op("vector", lambda v: v.tensor_reduce(SM[:, 43:44], UGC[:, 384:432],
                                                 AX.X, OP.max), sem="vv")
        c.enop("vector")
        partition_absmax_to_ccin(SM[:, 43:44], base + 3)
        bval = barrier()

        c.wait("vector", "dx", bval)
        c.op("vector", lambda v, b=base: v.tensor_scalar(
            SM[0:1, 7:8], CCOUT[0:1, b + 3:b + 4], 1.0 / 32767.0, 1e-12, OP.mult, OP.max))
        c.enop("vector")
        c.op("vector", lambda v: v.reciprocal(SM[0:1, 8:9], SM[0:1, 7:8]))
        c.enop("vector")
        c.op("vector", lambda v: v.tensor_tensor(SM[0:1, 9:10], SM[0:1, 7:8],
                                                 SM[0:1, 2:3], OP.mult), sem="vv")
        c.wait("tensor", "vv", c.counts["vv"])
        bcast_scalar(SM[0:1, 8:9], S2IC)
        bcast_scalar(SM[0:1, 9:10], S2UC)
        if stop_at == "C":
            _stop_copy(); return

        cd_guard = []
        d_prev_vv = 0
        for ti in range(NI * 2):
            ic, th = divmod(ti, 2)
            pin2 = BB[:, CD_IN + (ti % 2) * 2048: CD_IN + (ti % 2) * 2048 + 2048]
            uin = BB[:, CD_UP + (ti % 2) * 2048: CD_UP + (ti % 2) * 2048 + 2048]
            pout = BB[:, CD_OUT + (ti % 2) * 2048: CD_OUT + (ti % 2) * 2048 + 2048]
            if ti >= 2:
                gdst, ggv = cd_guard[ti - 2]
                c.wait("sync", "dst", gdst)
            c.dma("sync", pin2, L["sp"]["p2"][ic * 128:(ic + 1) * 128,
                                              th * 2048:(th + 1) * 2048], "dsp")
            c.dma("sync", uin, L["sp"]["up"][ic * 128:(ic + 1) * 128,
                                             th * 2048:(th + 1) * 2048], "dsp")
            dv = c.counts["dsp"]
            Y = BFA[:, 0:2048]
            G2 = BFA[:, 2048:4096]
            Y4 = BFA[:, 4096:6144]
            UQ = BFA[:, 6144:8192]
            c.wait("scalar", "dsp", dv)
            if ti >= 1:
                c.wait("scalar", "vv", d_prev_vv)
            c.op("scalar", lambda s_, pin2=pin2, Y=Y: s_.activation(
                Y, pin2, AF.Identity, bias=KCOL[:, 0:1], scale=sc(S2IC)), sem="ss")
            sv = c.counts["ss"]
            c.op("scalar", lambda s_, uin=uin, Y4=Y4: s_.activation(
                Y4, uin, AF.Identity, bias=KCOL[:, 0:1], scale=sc(SUIC)), sem="ss")
            sv2 = c.counts["ss"]
            c.wait("vector", "ss", sv2)
            c.op("vector", lambda v, Y=Y, G2=G2: v.tensor_scalar(
                G2, Y, KCOL[:, 0:1], None, OP.subtract))
            c.op("vector", lambda v, Y4=Y4, UQ=UQ: v.tensor_scalar(
                UQ, Y4, KCOL[:, 0:1], None, OP.subtract), sem="vv")
            vv1 = c.counts["vv"]
            d_prev_vv = vv1
            if ti >= 2:
                c.wait("vector", "dst", cd_guard[ti - 2][0])
            c.op("vector", lambda v, G2=G2, UQ=UQ, pout=pout: v.scalar_tensor_tensor(
                pout, G2, sc(S2UC), UQ, OP.mult, OP.mult), sem="gg")
            gv = c.counts["gg"]
            c.wait("vector", "gg", gv)
            c.op("vector", lambda v, pout=pout, ti=ti: v.tensor_reduce(
                UGC[:, 384 + ti:385 + ti], pout, AX.X, OP.max,
                apply_absolute_value=True), sem="vv")
            c.wait("sync", "gg", gv)
            c.dma("sync", L["sp"]["p3"][ic * 128:(ic + 1) * 128,
                                        th * 2048:(th + 1) * 2048], pout, "dst")
            cd_guard.append((c.counts["dst"], c.counts["vv"]))
        c.wait("vector", "vv", c.counts["vv"])
        c.op("vector", lambda v: v.tensor_reduce(SM[:, 44:45], UGC[:, 384:432],
                                                 AX.X, OP.max), sem="vv")
        c.enop("vector")
        partition_absmax_to_ccin(SM[:, 44:45], base + 4)
        bval = barrier()

        c.wait("vector", "dx", bval)
        c.op("vector", lambda v, b=base: v.tensor_scalar(
            SM[0:1, 10:11], CCOUT[0:1, b + 4:b + 5], 1.0 / 32767.0, 1e-12, OP.mult, OP.max))
        c.enop("vector")
        c.op("vector", lambda v: v.reciprocal(SM[0:1, 11:12], SM[0:1, 10:11]))
        c.enop("vector")
        c.op("vector", lambda v: v.tensor_scalar(SM[0:1, 12:13], SM[0:1, 10:11],
                                                 0.5, None, OP.mult), sem="vv")
        c.wait("tensor", "vv", c.counts["vv"])
        bcast_scalar(SM[0:1, 11:12], S3IC)
        bcast_scalar(SM[0:1, 12:13], S3HC)
        if stop_at == "D":
            _stop_copy(); return

        stopE = stop_at if (stop_at or "").startswith("E") else None
        if stopE != "EY":
            for ic in range(NI):
                c.dma("sync", BB[:, WDT + ic * 1024: WDT + (ic + 1) * 1024],
                      L["w_bf"]["w_down"][:, ic * 128:(ic + 1) * 128], "dw", transpose=True)
            dwv = c.counts["dw"]
            c.wait("sync", "dw", dwv)
        else:
            dwv = c.counts["dw"]
        prev_down_mm = None
        for q in (range(4) if stopE not in ("EZ", "EY") else []):
            or_guard = []
            if prev_down_mm is not None:
                c.wait("vector", "mm", prev_down_mm)
            for ic in range(NI):
                pin3 = BB[:, PIN + (ic % 2) * 1024: PIN + (ic % 2) * 1024 + 1024]
                if ic >= 2:
                    c.wait("sync", "vv", or_guard[ic - 2])
                c.dma("sync", pin3, L["sp"]["p3"][ic * 128:(ic + 1) * 128,
                                                  q * 1024:(q + 1) * 1024], "dsp")
                dv = c.counts["dsp"]
                Y = BFA[:, YF + (ic % 2) * 1024: YF + (ic % 2) * 1024 + 1024]
                c.wait("scalar", "dsp", dv)
                if ic >= 2:
                    c.wait("scalar", "vv", or_guard[ic - 2])
                c.op("scalar", lambda s_, pin3=pin3, Y=Y: s_.activation(
                    Y, pin3, AF.Identity, bias=KCOL[:, 0:1], scale=sc(S3IC)), sem="ss")
                sv = c.counts["ss"]
                c.wait("vector", "ss", sv)
                c.op("vector", lambda v, Y=Y, ic=ic: v.tensor_scalar(
                    BB[:, ORQ + ic * 1024: ORQ + (ic + 1) * 1024], Y, KCOL[:, 0:1],
                    None, OP.subtract), sem="vv")
                or_guard.append(c.counts["vv"])
            orq_vv = c.counts["vv"]
            if stopE == "E0":
                continue
            c.wait("tensor", "vv", orq_vv)
            if q == 0:
                c.wait("tensor", "dw", dwv)
            dn_ev = []
            for g in range(16):
                hc, tq = divmod(g, 2)
                half = g % 2
                pcol = PS_SC[:, half * 512:(half + 1) * 512]
                if g >= 2:
                    c.wait("tensor", "ev", dn_ev[g - 2])
                for ic in range(NI):
                    c.op("tensor", lambda t, hc=hc, tq=tq, ic=ic, pcol=pcol: t.matmul(
                        pcol,
                        BB[:, WDT + ic * 1024 + hc * 128: WDT + ic * 1024 + (hc + 1) * 128],
                        BB[:, ORQ + ic * 1024 + tq * 512: ORQ + ic * 1024 + (tq + 1) * 512],
                        start=(ic == 0), stop=(ic == NI - 1)),
                        sem="mm", inc=(1 if ic == NI - 1 else 0))
                mmv = c.counts["mm"]
                stg = BB[:, FFO + tq * 4096 + hc * 512: FFO + tq * 4096 + (hc + 1) * 512]
                c.wait("scalar", "mm", mmv)
                c.op("scalar", lambda s_, pcol=pcol, stg=stg: s_.activation(
                    stg, pcol, AF.Identity, scale=sc(S3HC)), sem="ev")
                dn_ev.append(c.counts["ev"])
            prev_down_mm = c.counts["mm"]
            ffo_ev = c.counts["ev"]
            if stopE == "E1":
                continue
            c.wait("tensor", "ev", ffo_ev)
            for jj in range(8):
                j = q * 8 + jj
                tq, sub = divmod(jj, 4)
                xb = BFA[:, XF + (jj % 2) * 1024: XF + (jj % 2) * 1024 + 1024]
                if jj >= 2 or q > 0:
                    c.wait("sync", "dst", c.counts["dst"])
                c.dma("sync", xb, src[j * 128:(j + 1) * 128, :], "dx")
                dxv = c.counts["dx"]
                for hc in range(ND):
                    c.op("tensor", lambda t, hc=hc, tq=tq, sub=sub: t.transpose(
                        PS_T[:, hc * 128:(hc + 1) * 128],
                        BB[:, FFO + tq * 4096 + hc * 512 + sub * 128:
                           FFO + tq * 4096 + hc * 512 + sub * 128 + 128],
                        L["ID"][:, :]), sem="mm", inc=(1 if hc == ND - 1 else 0))
                mmv = c.counts["mm"]
                c.wait("vector", "mm", mmv)
                c.wait("vector", "dx", dxv)
                c.op("vector", lambda v, xb=xb: v.tensor_tensor(
                    xb, xb, PS_T[:, :], OP.add), sem="vv")
                vvv = c.counts["vv"]
                c.wait("tensor", "vv", vvv)
                c.wait("sync", "vv", vvv)
                c.dma("sync", dst[j * 128:(j + 1) * 128, :], xb, "dst")
        if stopE in ("E0", "E1", "EZ", "EY"):
            _stop_copy()

    def attn(dst, dump_ot=False):
        fence()
        c.wait("sync", "dst", c.counts["dst"])
        for s in range(1 if dump_ot else NSEQ):
            soff = s * S
            rstd_ss = ln_stats(L["xs"], soff, 16, False)
            c.wait("vector", "ss", rstd_ss)
            c.op("vector", lambda v: v.scalar_tensor_tensor(
                STAT[:, 224:240], STAT[:, 128:144], -1.0, STAT[:, 160:176],
                OP.mult, OP.mult), sem="vv")
            bia_vv = c.counts["vv"]
            c.wait("scalar", "vv", bia_vv)
            ln_guard = []
            for j in range(16):
                jt = soff // 128 + j
                xb = BFA[:, XF2 + (j % 2) * 1024: XF2 + (j % 2) * 1024 + 1024]
                if j >= 2:
                    c.wait("sync", "ss", ln_guard[j - 2])
                c.dma("sync", xb, L["xs"][jt * 128:(jt + 1) * 128, :], "dx")
                dxv = c.counts["dx"]
                c.wait("scalar", "dx", dxv)
                c.op("scalar", lambda s_, xb=xb, j=j: s_.activation(
                    BB[:, AST:AST + 1024], xb, AF.Identity,
                    bias=STAT[:, 224 + j:225 + j], scale=STAT[:, 160 + j:161 + j]), sem="ss")
                sv = c.counts["ss"]
                c.wait("tensor", "ss", sv)
                for dc in range(ND):
                    c.op("tensor", lambda t, dc=dc: t.transpose(
                        PS_T[:, dc * 128:(dc + 1) * 128],
                        BB[:, AST + dc * 128: AST + (dc + 1) * 128], L["ID"][:, :]),
                        sem="mm", inc=(1 if dc == ND - 1 else 0))
                mmv = c.counts["mm"]
                c.wait("scalar", "mm", mmv)
                out3 = BB[:, LN2T:LN2T + 16384].rearrange("p (a b) -> p a b", a=8)[
                    :, :, j * 128:(j + 1) * 128]
                c.op("scalar", lambda s_, out3=out3: s_.copy(
                    out3, PS_T[:, :].rearrange("p (a b) -> p a b", a=8)), sem="ss")
                ln_guard.append(c.counts["ss"])
                c.wait("tensor", "ss", c.counts["ss"])
            ln2t_ss = c.counts["ss"]

            for wi, (wname, spt, bias_t) in enumerate(
                    (("Wq", L["qsp"], L["BQC"]), ("Wk", L["ksp"], L["BKC"]))):
                for whalf in range(2):
                    c.wait("sync", "mm", c.counts["mm"])
                    for dc in range(ND):
                        c.dma("sync", BB[:, WAT + dc * 512: WAT + (dc + 1) * 512],
                              L["w_bf"][wname][dc * 128:(dc + 1) * 128,
                                               whalf * 512:(whalf + 1) * 512], "dw")
                    dwv = c.counts["dw"]
                    c.wait("tensor", "dw", dwv)
                    c.wait("tensor", "ss", ln2t_ss)
                    g_ev, g_dst, g_vv = [], [], []
                    for g in range(16):
                        oc4, qb = divmod(g, 4)
                        oc = whalf * 4 + oc4
                        half = g % 2
                        pcol = PS_SC[:, half * 512:(half + 1) * 512]
                        stg = BB[:, AST + half * 512: AST + half * 512 + 512]
                        if g >= 2:
                            c.wait("tensor", "ev", g_ev[g - 2])
                        for dc in range(ND):
                            c.op("tensor", lambda t, oc4=oc4, qb=qb, dc=dc, pcol=pcol: t.matmul(
                                pcol,
                                BB[:, WAT + dc * 512 + oc4 * 128:
                                   WAT + dc * 512 + (oc4 + 1) * 128],
                                BB[:, LN2T + dc * 2048 + qb * 512:
                                   LN2T + dc * 2048 + (qb + 1) * 512],
                                start=(dc == 0), stop=(dc == ND - 1)),
                                sem="mm", inc=(1 if dc == ND - 1 else 0))
                        mmv = c.counts["mm"]
                        c.wait("scalar", "mm", mmv)
                        if g >= 2:
                            c.wait("scalar", "dst", g_dst[g - 2])
                            c.wait("scalar", "vv", g_vv[g - 2])
                        c.op("scalar", lambda s_, pcol=pcol, stg=stg, oc=oc, bias_t=bias_t:
                             s_.activation(stg, pcol, AF.Identity,
                                           bias=bias_t[:, oc:oc + 1]), sem="ev")
                        evv = c.counts["ev"]
                        g_ev.append(evv)
                        c.wait("vector", "ev", evv)
                        cidx = wi * 32 + oc * 4 + qb
                        c.op("vector", lambda v, stg=stg, cidx=cidx: v.tensor_reduce(
                            UGC[:, cidx:cidx + 1], stg, AX.X, OP.max,
                            apply_absolute_value=True), sem="vv")
                        g_vv.append(c.counts["vv"])
                        c.wait("sync", "ev", evv)
                        c.dma("sync", spt[oc * 128:(oc + 1) * 128,
                                          qb * 512:(qb + 1) * 512], stg, "dst")
                        g_dst.append(c.counts["dst"])
            qk_dst = c.counts["dst"]
            qk_vv = c.counts["vv"]

            for whalf in range(2):
                c.wait("sync", "mm", c.counts["mm"])
                for dc in range(ND):
                    c.dma("sync", BB[:, WAT + dc * 512: WAT + (dc + 1) * 512],
                          L["w_bf"]["Wv"][dc * 128:(dc + 1) * 128,
                                          whalf * 512:(whalf + 1) * 512], "dw")
                dwv = c.counts["dw"]
                c.wait("tensor", "dw", dwv)
                c.wait("tensor", "ss", ln2t_ss)
                v_ev = []
                for tt in range(16):
                    half = tt % 2
                    pcol = PS_SC[:, half * 512:(half + 1) * 512]
                    if tt >= 2:
                        c.wait("tensor", "ev", v_ev[tt - 2])
                    for dc in range(ND):
                        c.op("tensor", lambda t, tt=tt, dc=dc, pcol=pcol: t.matmul(
                            pcol,
                            BB[:, LN2T + dc * 2048 + tt * 128:
                               LN2T + dc * 2048 + (tt + 1) * 128],
                            BB[:, WAT + dc * 512: WAT + (dc + 1) * 512],
                            start=(dc == 0), stop=(dc == ND - 1)),
                            sem="mm", inc=(1 if dc == ND - 1 else 0))
                    mmv = c.counts["mm"]
                    c.wait("vector", "mm", mmv)
                    vp3 = BB[:, VP + tt * 1040: VP + tt * 1040 + 1040].rearrange(
                        "p (a b) -> p a b", a=16)[:, whalf * 8:(whalf + 1) * 8, 0:64]
                    c.op("vector", lambda v, vp3=vp3, pcol=pcol, whalf=whalf: v.tensor_tensor(
                        vp3, pcol.rearrange("p (a b) -> p a b", a=8),
                        L["BVROW"][:, whalf * 512:(whalf + 1) * 512].rearrange(
                            "p (a b) -> p a b", a=8), OP.add), sem="ev")
                    v_ev.append(c.counts["ev"])
            vdone_ev = c.counts["ev"]
            c.wait("vector", "ev", vdone_ev)
            ones3 = BB[:, VP:VP + 16640].rearrange("p (a b) -> p a b", a=256)[:, :, 64:65]
            c.op("vector", lambda v, ones3=ones3: v.memset(ones3, 1.0), sem="vv")
            vp_vv = c.counts["vv"]

            c.wait("vector", "vv", vp_vv)
            c.op("vector", lambda v: v.tensor_reduce(
                SM[:, 0:8], UGC[:, 0:32].rearrange("p (a b) -> p a b", a=8),
                AX.X, OP.max), sem="vv")
            c.enop("vector")
            c.op("vector", lambda v: v.tensor_reduce(
                SM[:, 8:16], UGC[:, 32:64].rearrange("p (a b) -> p a b", a=8),
                AX.X, OP.max), sem="vv")
            c.enop("vector")
            c.op("vector", lambda v: v.tensor_copy(BB[:, AST:AST + 16], SM[:, 0:16]),
                 sem="vv")
            c.wait("tensor", "vv", c.counts["vv"])
            c.op("tensor", lambda t: t.transpose(PS_T[0:16, 0:128], BB[:, AST:AST + 16],
                                                 L["ID"][:, :]), sem="mm")
            c.wait("vector", "mm", c.counts["mm"])
            c.op("vector", lambda v: v.tensor_reduce(
                SM[0:16, 16:17], PS_T[0:16, 0:64], AX.X, OP.max), sem="vv")
            c.enop("vector")
            c.op("vector", lambda v: v.tensor_reduce(
                SM[0:16, 17:18], PS_T[0:16, 64:128], AX.X, OP.max), sem="vv")
            mx_vv = c.counts["vv"]
            c.wait("tensor", "vv", mx_vv)
            c.wait("sync", "vv", mx_vv)
            c.dma("sync", SM[0:8, 20:22], SM[8:16, 16:18], "dx")
            dxv = c.counts["dx"]
            c.wait("vector", "dx", dxv)
            c.op("vector", lambda v: v.scalar_tensor_tensor(
                SM[0:8, 22:24], SM[0:8, 16:18], -8.0, SM[0:8, 20:22],
                OP.mult, OP.mult))
            c.enop("vector")
            c.op("vector", lambda v: v.tensor_scalar(
                SM[0:8, 22:24], SM[0:8, 22:24], 45.0, None, OP.add), sem="vv")
            negm_vv = c.counts["vv"]
            c.wait("sync", "vv", negm_vv)
            c.dma("sync", SM[0:1, 48:64], SM[0:8, 22:24], "dx")
            dxv = c.counts["dx"]
            c.wait("tensor", "dx", dxv)
            c.op("tensor", lambda t: t.matmul(PS_X[:, 0:16], ONES[0:1, :],
                                              SM[0:1, 48:64], start=True, stop=True),
                 sem="mm")
            c.wait("scalar", "mm", c.counts["mm"])
            c.op("scalar", lambda s_: s_.copy(L["NEGM"][:, :], PS_X[:, 0:16]), sem="ss")
            negm_ss = c.counts["ss"]
            c.wait("tensor", "ss", negm_ss)

            c.wait("sync", "dst", qk_dst)
            head_mm_end = []
            for h in range(H):
                qb_buf = BB[:, QKH + (h % 2) * 4096: QKH + (h % 2) * 4096 + 2048]
                kb_buf = BB[:, QKH + (h % 2) * 4096 + 2048: QKH + (h % 2) * 4096 + 4096]
                if h >= 2:
                    c.wait("sync", "mm", head_mm_end[h - 2])
                c.dma("sync", qb_buf[0:64, :], L["qsp"][h * 64:(h + 1) * 64, :], "dsp")
                c.dma("sync", kb_buf[0:64, :], L["ksp"][h * 64:(h + 1) * 64, :], "dsp")
                dqk = c.counts["dsp"]
                c.wait("tensor", "dsp", dqk)
                c.wait("tensor", "ss", negm_ss)
                prev_exp_ss = None
                for kc in range(16):
                    q0 = kc * 128
                    win = 0
                    while q0 + win * 1024 < S:
                        wa = q0 + win * 1024
                        wl = min(1024, S - wa)
                        if prev_exp_ss is not None:
                            c.wait("tensor", "ss", prev_exp_ss)
                        pieces = []
                        if win == 0:
                            pieces.append((0, 128))
                            pieces += bank_splits(128, wl)
                        else:
                            pieces = bank_splits(0, wl)
                        for pi, (pa, pl) in enumerate(pieces):
                            c.op("tensor", lambda t, kc=kc, wa=wa, pa=pa, pl=pl,
                                 qb_buf=qb_buf, kb_buf=kb_buf: t.matmul(
                                     PS_SC[:, pa:pa + pl],
                                     kb_buf[0:64, kc * 128:(kc + 1) * 128],
                                     qb_buf[0:64, wa + pa:wa + pa + pl],
                                     start=True, stop=True),
                                 sem="mm", inc=(1 if pi == len(pieces) - 1 else 0))
                        mmv = c.counts["mm"]
                        if win == 0:
                            c.wait("vector", "mm", mmv)
                            c.op("vector", lambda v: v.tensor_tensor(
                                PS_SC[:, 0:128], PS_SC[:, 0:128], L["MASK"][:, :],
                                OP.add), sem="vv")
                            c.wait("scalar", "vv", c.counts["vv"])
                        else:
                            c.wait("scalar", "mm", mmv)
                        c.op("scalar", lambda s_, wl=wl, h=h: s_.activation(
                            BB[:, PRB:PRB + wl], PS_SC[:, 0:wl], AF.Exp, scale=0.125,
                            bias=L["NEGM"][:, h:h + 1]), sem="ss")
                        prev_exp_ss = c.counts["ss"]
                        c.wait("tensor", "ss", prev_exp_ss)
                        for pa, pl in bank_splits(wa, wa + wl):
                            c.op("tensor", lambda t, kc=kc, wa=wa, pa=pa, pl=pl, h=h: t.matmul(
                                PS_PV[0:65, pa:pa + pl],
                                BB[:, VP + kc * 1040 + h * 65: VP + kc * 1040 + (h + 1) * 65],
                                BB[:, PRB + (pa - wa):PRB + (pa - wa) + pl],
                                start=(kc == 0), stop=(kc == 15)),
                                sem="mm", inc=0)
                        c.op("tensor", lambda t: t.nop(), sem="mm")
                        win += 1
                head_mm_end.append(c.counts["mm"])
                pv_mm = c.counts["mm"]
                c.wait("scalar", "mm", pv_mm)
                c.op("scalar", lambda s_: s_.copy(BFA[64:65, YF:YF + S],
                                                  PS_PV[64:65, :]), sem="ss")
                lv = c.counts["ss"]
                c.wait("sync", "ss", lv)
                c.dma("sync", L["INVL"][h:h + 1, :], BFA[64:65, YF:YF + S], "dx")
                oc = h // 2
                odd = h % 2
                for qb in range(4):
                    if odd:
                        otdst = BB[0:64, AST + qb * 512: AST + (qb + 1) * 512]
                    else:
                        otdst = BB[0:64, OT + oc * 2048 + qb * 512:
                                   OT + oc * 2048 + (qb + 1) * 512]
                    c.op("scalar", lambda s_, otdst=otdst, qb=qb: s_.copy(
                        otdst, PS_PV[0:64, qb * 512:(qb + 1) * 512]), sem="ev")
                ot_ev = c.counts["ev"]
                c.wait("tensor", "ev", ot_ev)
                if odd:
                    c.wait("sync", "ev", ot_ev)
                    c.dma("sync", BB[64:128, OT + oc * 2048: OT + (oc + 1) * 2048],
                          BB[0:64, AST:AST + 2048], "dst")
                    c.wait("scalar", "dst", c.counts["dst"])

            c.wait("vector", "dx", c.counts["dx"])
            c.op("vector", lambda v: v.reciprocal(BFA[0:16, YF:YF + S],
                                                  L["INVL"][0:16, :]))
            c.enop("vector")
            c.op("vector", lambda v: v.tensor_tensor(
                BFA[0:16, YF + S:YF + 2 * S], L["INVL"][0:16, :],
                BFA[0:16, YF:YF + S], OP.mult))
            c.enop("vector")
            c.op("vector", lambda v: v.tensor_scalar(
                BFA[0:16, YF + S:YF + 2 * S], BFA[0:16, YF + S:YF + 2 * S],
                -1.0, 2.0, OP.mult, OP.add))
            c.enop("vector")
            c.op("vector", lambda v: v.tensor_tensor(
                L["INVL"][0:16, :], BFA[0:16, YF:YF + S],
                BFA[0:16, YF + S:YF + 2 * S], OP.mult), sem="vv")
            invl_vv = c.counts["vv"]
            c.wait("tensor", "vv", invl_vv)
            for oc in range(ND):
                for qb in range(4):
                    c.op("tensor", lambda t, oc=oc, qb=qb: t.matmul(
                        PS_X[:, 0:512], L["SELP"][:, oc * 128:(oc + 1) * 128],
                        L["INVL"][0:16, qb * 512:(qb + 1) * 512],
                        start=True, stop=True), sem="mm")
                    mmv = c.counts["mm"]
                    c.wait("scalar", "mm", mmv)
                    c.op("scalar", lambda s_: s_.copy(BFA[:, 0:512],
                                                      PS_X[:, 0:512]), sem="ss")
                    sv = c.counts["ss"]
                    c.wait("vector", "ss", sv)
                    otsl = BB[:, OT + oc * 2048 + qb * 512: OT + oc * 2048 + (qb + 1) * 512]
                    c.op("vector", lambda v, otsl=otsl: v.tensor_tensor(
                        otsl, otsl, BFA[:, 0:512], OP.mult), sem="ev")
                    evv = c.counts["ev"]
                    c.wait("tensor", "ev", evv)
                    c.wait("scalar", "ev", evv)
            otn_ev = c.counts["ev"]
            c.wait("tensor", "ev", otn_ev)
            if dump_ot:
                c.wait("sync", "ev", otn_ev)
                for ch in range(4):
                    c.op("scalar", lambda s_, ch=ch: s_.copy(
                        BFA[:, 0:4096], BB[:, OT + ch * 4096: OT + (ch + 1) * 4096]),
                        sem="ss")
                    c.wait("sync", "ss", c.counts["ss"])
                    c.dma("sync", bass.AP(L["y_out"].tensor, ch * 512 * 1024,
                                          [[1024, 512], [1, 1024]]), BFA[:, 0:4096], "dst")
                    c.wait("scalar", "dst", c.counts["dst"])
                c.dma("sync", bass.AP(L["y_out"].tensor, 2048 * 1024,
                                      [[1024, 32], [1, 1024]]), L["INVL"][:, :], "dst")
                c.wait("sync", "dst", c.counts["dst"])
                return

            fence()
            for dc in range(ND):
                c.dma("sync", BB[:, WMQ + dc * 1024: WMQ + (dc + 1) * 1024],
                      L["w_bf"]["Wm"][dc * 128:(dc + 1) * 128, :], "dw")
            dwv = c.counts["dw"]
            c.wait("tensor", "dw", dwv)
            for qb in range(4):
                pj_ev = []
                for oc in range(ND):
                    half = oc % 2
                    pcol = PS_SC[:, half * 512:(half + 1) * 512]
                    if oc >= 2:
                        c.wait("tensor", "ev", pj_ev[oc - 2])
                    for dc in range(ND):
                        c.op("tensor", lambda t, oc=oc, qb=qb, dc=dc, pcol=pcol: t.matmul(
                            pcol,
                            BB[:, WMQ + dc * 1024 + oc * 128:
                               WMQ + dc * 1024 + (oc + 1) * 128],
                            BB[:, OT + dc * 2048 + qb * 512:
                               OT + dc * 2048 + (qb + 1) * 512],
                            start=(dc == 0), stop=(dc == ND - 1)),
                            sem="mm", inc=(1 if dc == ND - 1 else 0))
                    mmv = c.counts["mm"]
                    stg = BB[:, APQ + oc * 512: APQ + (oc + 1) * 512]
                    c.wait("scalar", "mm", mmv)
                    c.op("scalar", lambda s_, pcol=pcol, stg=stg, oc=oc: s_.activation(
                        stg, pcol, AF.Identity, bias=L["BMC"][:, oc:oc + 1]), sem="ev")
                    pj_ev.append(c.counts["ev"])
                apq_ev = c.counts["ev"]
                c.wait("tensor", "ev", apq_ev)
                for jj in range(4):
                    j = soff // 128 + qb * 4 + jj
                    xb = BFA[:, XF + (jj % 2) * 1024: XF + (jj % 2) * 1024 + 1024]
                    if qb > 0 or jj >= 2:
                        c.wait("sync", "dst", c.counts["dst"])
                    c.dma("sync", xb, L["xs"][j * 128:(j + 1) * 128, :], "dx")
                    dxv = c.counts["dx"]
                    for oc in range(ND):
                        c.op("tensor", lambda t, oc=oc, jj=jj: t.transpose(
                            PS_T[:, oc * 128:(oc + 1) * 128],
                            BB[:, APQ + oc * 512 + jj * 128:
                               APQ + oc * 512 + jj * 128 + 128],
                            L["ID"][:, :]), sem="mm", inc=(1 if oc == ND - 1 else 0))
                    mmv = c.counts["mm"]
                    c.wait("vector", "mm", mmv)
                    c.wait("vector", "dx", dxv)
                    c.op("vector", lambda v, xb=xb: v.tensor_tensor(
                        xb, xb, PS_T[:, :], OP.add), sem="vv")
                    vvv = c.counts["vv"]
                    c.wait("tensor", "vv", vvv)
                    c.wait("sync", "vv", vvv)
                    c.dma("sync", dst[j * 128:(j + 1) * 128, :], xb, "dst")

    if upto == "dumpB":
        ffn(0, L["x_in"], L["y_out"], stop_at="B")
        c.wait("sync", "dst", c.counts["dst"])
        for ic in range(8):
            buf = BB[:, 0:4096]
            f32buf = BFA[:, 0:4096]
            c.dma("sync", buf, L["sp"]["up"][ic * 128:(ic + 1) * 128, :], "dsp")
            c.wait("scalar", "dsp", c.counts["dsp"])
            c.op("scalar", lambda s_: s_.copy(f32buf, buf), sem="ss")
            c.wait("sync", "ss", c.counts["ss"])
            c.dma("sync", bass.AP(L["y_out"].tensor, ic * 512 * 1024,
                                  [[1024, 512], [1, 1024]]), f32buf, "dst")
            c.wait("sync", "dst", c.counts["dst"])
        c.wait("sync", "dst", c.counts["dst"])
        return
    if upto in ("p0", "ffnA", "ffnB", "ffnC", "ffnD", "ffnE0", "ffnE1", "ffnEZ", "ffnEY"):
        if upto == "p0":
            for j in range(NT):
                c.dma("sync", L["y_out"][j * 128:(j + 1) * 128, :],
                      L["x_in"][j * 128:(j + 1) * 128, :], "dst")
        else:
            ffn(0, L["x_in"], L["y_out"], stop_at=upto[3:])
    elif upto == "ffn1":
        ffn(0, L["x_in"], L["y_out"])
    elif upto == "attn":
        ffn(0, L["x_in"], L["xs"])
        attn(L["y_out"])
    elif upto == "dumpOT":
        ffn(0, L["x_in"], L["xs"])
        attn(L["y_out"], dump_ot=True)
    else:
        ffn(0, L["x_in"], L["xs"])
        attn(L["xs"])
        ffn(1, L["xs"], L["y_out"])
    c.wait("sync", "dst", c.counts["dst"])



def _consts():
    ident_bf = np.eye(128, dtype=ml_dtypes.bfloat16)
    m = np.zeros((128, 128), dtype=np.float32)
    m[np.arange(128)[:, None] > np.arange(128)[None, :]] = -1e12
    selh = np.zeros((16, H * HS), dtype=np.float32)
    for h in range(H):
        selh[h, h * HS:(h + 1) * HS] = 1.0
    selp = np.zeros((16, 8 * 128), dtype=np.float32)
    for oc in range(8):
        selp[2 * oc, oc * 128:oc * 128 + 64] = 1.0
        selp[2 * oc + 1, oc * 128 + 64:oc * 128 + 128] = 1.0
    return ident_bf, m, selh, selp


def _in_maps(x, Wq, bq, Wk, bk, Wv, bv, Wm, bm, w_gate, w_up, w_down):
    ident_bf, maskt, selh, selp = _consts()
    f32 = lambda a: np.ascontiguousarray(np.asarray(a, dtype=np.float32))
    base = dict(Wq=f32(Wq), bq=f32(bq), Wk=f32(Wk), bk=f32(bk), Wv=f32(Wv),
                bv=f32(bv), Wm=f32(Wm), bm=f32(bm), w_gate=f32(w_gate),
                w_up=f32(w_up), w_down=f32(w_down),
                ident_bf=ident_bf, maskt=maskt, selh=selh, selp=selp)
    maps = []
    for core in range(8):
        m = dict(base)
        m["x"] = np.ascontiguousarray(
            np.asarray(x, dtype=np.float32)[2 * core:2 * core + 2].reshape(T, D))
        maps.append(m)
    return maps


def kernel(L0_attn_x, L0_attn_Wq, L0_attn_bq, L0_attn_Wk, L0_attn_bk,
           L0_attn_Wv, L0_attn_bv, L0_attn_Wm, L0_attn_bm,
           L0_ffn_w_gate, L0_ffn_w_up, L0_ffn_w_down):
    nc = build_nc()
    maps = _in_maps(L0_attn_x, L0_attn_Wq, L0_attn_bq, L0_attn_Wk, L0_attn_bk,
                    L0_attn_Wv, L0_attn_bv, L0_attn_Wm, L0_attn_bm,
                    L0_ffn_w_gate, L0_ffn_w_up, L0_ffn_w_down)
    res = run_bass_kernel_spmd(nc, maps, core_ids=list(range(8)))
    out = np.empty((16, 2048, 1024), dtype=np.float32)
    for core in range(8):
        out[2 * core:2 * core + 2] = np.asarray(
            res.results[core]["y"]).reshape(2, 2048, 1024)
    return out
